# version 13
# baseline (speedup 1.0000x reference)
import math
import numpy as np
import concourse.bass as bass
import concourse.mybir as mybir
from concourse.bass_utils import run_bass_kernel_spmd
from concourse.bass import IndirectOffsetOnAxis

F32 = mybir.dt.float32
BF16 = mybir.dt.bfloat16
I32 = mybir.dt.int32
AF = mybir.ActivationFunctionType
ALU = mybir.AluOpType
AX = mybir.AxisListType

P = 128
NEG = -30000.0


class Sched:

    ENGINES = ("te", "act", "dve", "pool", "sync")

    def __init__(self, nc, ctx):
        self.nc = nc
        self.ctx = ctx
        self.pend = {e: [] for e in self.ENGINES}
        self.sems = {e: ctx.enter_context(nc.semaphore(f"tl_{e}")) for e in self.ENGINES}
        self.own = set(s.num for s in self.sems.values())
        self.cnt = {e: 0 for e in self.ENGINES}
        self.dma_sems = {}
        self.dma_cnt = {}
        self.waited = {e: {} for e in self.ENGINES}

    def dma_sem(self, name):
        if name not in self.dma_sems:
            self.dma_sems[name] = self.ctx.enter_context(self.nc.semaphore("ds_" + name))
            self.dma_cnt[name] = 0
        return self.dma_sems[name]

    @staticmethod
    def _flat(deps):
        out = []
        for d in deps:
            if d is None:
                continue
            if isinstance(d, list):
                out.extend(x for x in d if x is not None)
            else:
                out.append(d)
        return out

    def op(self, eng, emit, deps=(), tick=True):
        deps = self._flat(deps)
        h = None
        if tick:
            self.cnt[eng] += 1
            h = (self.sems[eng], self.cnt[eng])
        self.pend[eng].append((deps, emit, h))
        return h

    def dma(self, eng, emit, deps=(), sem="dma"):
        deps = self._flat(deps)
        s = self.dma_sem(sem)
        self.dma_cnt[sem] += 16
        h = (s, self.dma_cnt[sem])
        self.pend[eng].append((deps, emit, h))
        return h

    def barriers(self):
        hs = [(self.sems[e], self.cnt[e]) for e in self.ENGINES if self.cnt[e]]
        hs += [(self.dma_sems[n], self.dma_cnt[n]) for n in self.dma_sems]
        return hs

    def flush(self, block):
        for eng in self.ENGINES:
            ops = self.pend[eng]
            if not ops:
                continue
            self.pend[eng] = []
            dec = {"te": block.tensor, "act": block.scalar, "dve": block.vector,
                   "pool": block.gpsimd, "sync": block.sync}[eng]
            waited = self.waited[eng]
            own = self.own

            def body(E, ops=ops, waited=waited, own=own):
                for deps, emit, h in ops:
                    bysem = {}
                    for (sem, val) in deps:
                        if bysem.get(sem.num, (None, -1))[1] < val:
                            bysem[sem.num] = (sem, val)
                    for key, (sem, val) in sorted(bysem.items()):
                        if waited.get(key, -1) >= val:
                            continue
                        E.wait_ge(sem, val)
                        waited[key] = val
                    inst = emit(E)
                    if h is not None:
                        sem, _ = h
                        inst.then_inc(sem, 1 if sem.num in own else 16)
            dec(body)


def _identity(S, ap, deps=()):
    h0 = S.op("pool", lambda g, ap=ap: g.memset(ap, 0.0), deps=deps)
    return S.op("pool", lambda g, ap=ap: g.affine_select(
        out=ap, in_=ap, compare_op=ALU.not_equal, fill=1.0,
        base=0, pattern=[[-1, ap.shape[1]]], channel_multiplier=1), deps=[h0])


def emit_attn(nc, S, ctx, xt, xv, oo, U, L, QC, NSLOT=8, o_bf16=False,
              unit_sems=False, bg_hook=None, unit_end_cb=None):
    KPQ = QC // P
    NQC = L // QC
    OFF = P * (KPQ - 1)
    SCALE = 1.0 / math.sqrt(128.0)
    NVD = max(1, L // 1024)
    e = ctx.enter_context

    kt = [e(nc.sbuf_tensor(f"kt{i}", [P, L], BF16)) for i in range(2)]
    vt = [e(nc.sbuf_tensor(f"vt{i}", [P, L], BF16)) for i in range(2)]
    pring = e(nc.sbuf_tensor("pring", [P, NSLOT * QC], BF16))
    maskM = e(nc.sbuf_tensor("maskM", [P, OFF + QC], BF16))
    identb = e(nc.sbuf_tensor("identb", [P, P], BF16))
    identff = e(nc.sbuf_tensor("identff", [P, P], F32))
    identf1 = e(nc.sbuf_tensor("identf1", [1, 1], F32))
    onesb = e(nc.sbuf_tensor("onesb", [P, 1], BF16))
    lrow = [e(nc.sbuf_tensor(f"lrow{i}", [1, QC], F32)) for i in range(2)]
    recipT = [e(nc.sbuf_tensor(f"recipT{i}", [P, KPQ], F32)) for i in range(2)]
    osb = [e(nc.sbuf_tensor(f"osb{i}", [P, QC], F32)) for i in range(2)]
    osta = [e(nc.sbuf_tensor(f"osta{i}", [P, KPQ * P], BF16 if o_bf16 else F32))
            for i in range(2)]

    NSB = 4
    pss = [e(nc.psum_tensor(f"pss{i}", [P, QC], F32))[:] for i in range(NSB)]
    pso = e(nc.psum_tensor("pso", [P, QC], F32))
    psl = e(nc.psum_tensor("psl", [1, QC], F32))
    pstb = e(nc.psum_tensor("pstb", [P, KPQ * P], F32))
    pslt = e(nc.psum_tensor("pslt", [P, KPQ], F32))

    consts = [
        _identity(S, identb[:]),
        _identity(S, identff[:]),
        S.op("pool", lambda g: g.memset(onesb[:], 1.0)),
        S.op("pool", lambda g: g.memset(identf1[:], 1.0)),
    ]
    h_m0 = S.op("pool", lambda g: g.memset(maskM[:], 0.0))
    consts.append(S.op("pool", lambda g: g.affine_select(
        out=maskM[:], in_=maskM[:], compare_op=ALU.is_ge, fill=NEG,
        base=-OFF, pattern=[[1, OFF + QC]], channel_multiplier=-1), deps=[h_m0]))

    def kbs(qc):
        return KPQ * (qc + 1)

    h_kv = {}
    unit_done = {}

    def load_unit(u):
        deps = [unit_done[u - 2]] if u >= 2 else []
        hs = [S.dma("pool", lambda g, u=u: g.dma_start(
            out=kt[u % 2][:], in_=xt[u]), deps=deps, sem=f"kv{u % 2}")]
        vsrc = xv[u].rearrange("(nd kb p) d -> nd p kb d", p=P, nd=NVD)
        LG = L // NVD
        for i in range(NVD):
            vdst = vt[u % 2][:, i * LG:(i + 1) * LG].rearrange(
                "p (kb d) -> p kb d", d=P)
            hs.append(S.dma("pool", lambda g, s=vsrc[i], d=vdst: g.dma_start(
                out=d, in_=s), deps=deps, sem=f"kv{u % 2}"))
        h_kv[u] = hs

    load_unit(0)
    if U > 1:
        load_unit(1)

    kseq = [(u, qc, kb) for u in range(U) for qc in range(NQC)
            for kb in range(kbs(qc))]
    kidx = {key: i for i, key in enumerate(kseq)}
    chunks = [(u, qc) for u in range(U) for qc in range(NQC)]
    cidx = {c: i for i, c in enumerate(chunks)}

    smm, expd, pvd = {}, {}, {}
    evac = {}
    last = {}

    def emit_scores(key):
        u, qc, kb = key
        idx = kidx[key]
        sslot = idx % NSB
        deps = list(consts) + h_kv[u]
        if idx >= NSB:
            deps.append(expd[kseq[idx - NSB]])
        diag = kb >= KPQ * qc
        K = kt[u % 2]

        def em(t, K=K, sslot=sslot, kb=kb, qc=qc, diag=diag):
            mm = t.matmul(pss[sslot], K[:, kb * P:(kb + 1) * P],
                          K[:, qc * QC:(qc + 1) * QC], start=True, stop=not diag)
            if diag:
                w0 = OFF - P * (kb - KPQ * qc)
                mm = t.matmul(pss[sslot], identb[:],
                              maskM[:, w0:w0 + QC], start=False, stop=True)
            return mm
        smm[key] = S.op("te", em, deps=deps)

    def emit_exp(key):
        idx = kidx[key]
        sslot = idx % NSB
        slot = idx % NSLOT
        deps = [smm[key]]
        if idx >= NSLOT:
            deps.append(pvd[kseq[idx - NSLOT]])

        def em(s, sslot=sslot, slot=slot):
            return s.activation(out=pring[:, slot * QC:(slot + 1) * QC],
                                in_=pss[sslot], func=AF.Exp, scale=SCALE)
        expd[key] = S.op("act", em, deps=deps)

    def emit_lpv(key):
        u, qc, kb = key
        idx = kidx[key]
        slot = idx % NSLOT
        deps = [expd[key]]
        if kb == 0:
            deps.append(last.get(("psl",)))
            deps.append(last.get(("pso",)))
        lastkb = kb == kbs(qc) - 1

        def em(t, u=u, kb=kb, slot=slot, lastkb=lastkb):
            pslice = pring[:, slot * QC:(slot + 1) * QC]
            t.matmul(psl[:], onesb[:], pslice, start=kb == 0, stop=lastkb)
            return t.matmul(pso[:], vt[u % 2][:, kb * P:(kb + 1) * P],
                            pslice, start=kb == 0, stop=lastkb)
        pvd[key] = S.op("te", em, deps=deps)

    def emit_epilogue_evac(u, qc):
        lastkey = (u, qc, kbs(qc) - 1)
        par = qc % 2
        h_lcp = S.op("dve", lambda v, par=par: v.tensor_copy(
            lrow[par][:], psl[:]),
            deps=[pvd[lastkey], last.get(("lrow", par))])
        last[("psl",)] = h_lcp
        h_oev = S.op("dve", lambda v, par=par: v.tensor_copy(
            osb[par][:], pso[:]),
            deps=[pvd[lastkey], last.get(("osb", par))])
        last[("pso",)] = h_oev
        evac[(u, qc)] = (h_lcp, h_oev)

    def emit_epilogue(u, qc):
        par = qc % 2
        h_lcp, h_oev = evac[(u, qc)]
        tr_l = []
        for j in range(KPQ):
            tr_l.append(S.op("te", lambda t, j=j, par=par: t.transpose(
                pslt[:, j:j + 1], lrow[par][0:1, j * P:(j + 1) * P],
                identf1[0:1, 0:1]),
                deps=[h_lcp, last.get(("pslt",))]))
        last[("lrow", par)] = tr_l[-1]
        h_rec = S.op("dve", lambda v, par=par: v.reciprocal(
            recipT[par][:], pslt[:]),
            deps=tr_l + [last.get(("recipT", par))])
        last[("pslt",)] = h_rec
        htr = None
        for j in range(KPQ):
            htr = S.op("te", lambda t, j=j, par=par: t.transpose(
                pstb[:, j * P:(j + 1) * P], osb[par][:, j * P:(j + 1) * P],
                identff[:]),
                deps=[h_oev, last.get(("pstb",))])
        nrm = []
        for j in range(KPQ):
            hn = S.op("dve", lambda v, j=j, par=par: v.tensor_scalar_mul(
                osta[par][:, j * P:(j + 1) * P], pstb[:, j * P:(j + 1) * P],
                recipT[par][:, j:j + 1]),
                deps=[htr, h_rec, last.get(("osta", par))])
            nrm.append(hn)
        last[("pstb",)] = nrm[-1]
        last[("osb", par)] = nrm[-1]
        last[("recipT", par)] = nrm[-1]
        sem_name = f"ou{u}_{par}" if unit_sems else f"out{par}"
        if oo.shape and len(oo.shape) == 3:
            dst = oo[u, qc * QC:(qc + 1) * QC, :]
        else:
            dst = oo[u * L + qc * QC:u * L + (qc + 1) * QC, :]
        h_out = S.dma("sync", lambda sy, dst=dst, par=par: sy.dma_start(
            out=dst.rearrange("(t p) d -> p t d", p=P),
            in_=osta[par].ap().rearrange("p (t d) -> p t d", d=P)),
            deps=nrm, sem=sem_name)
        last[("osta", par)] = h_out
        out_handles.append(h_out)
        unit_outs.setdefault(u, []).append(h_out)
        if len(unit_outs[u]) == NQC and unit_end_cb is not None:
            unit_end_cb(u, list(unit_outs[u]))

    out_handles = []
    unit_outs = {}

    LAG = 3
    pending_epi = []

    def after_lpv(pkey):
        pu, pq, pk = pkey
        if pk == kbs(pq) - 1:
            emit_epilogue_evac(pu, pq)
            if bg_hook is not None:
                bg_hook()
            pending_epi.append((pu, pq))
            if pq == NQC - 1:
                unit_done[pu] = pvd[pkey]
                if pu + 2 < U:
                    load_unit(pu + 2)

    for i, key in enumerate(kseq):
        emit_scores(key)
        emit_exp(key)
        if i >= LAG:
            emit_lpv(kseq[i - LAG])
            after_lpv(kseq[i - LAG])
        while pending_epi:
            pu, pq = pending_epi[0]
            if kidx[key] - kidx[(pu, pq, kbs(pq) - 1)] >= LAG + 2:
                emit_epilogue(*pending_epi.pop(0))
            else:
                break
    for i in range(max(0, len(kseq) - LAG), len(kseq)):
        emit_lpv(kseq[i])
        after_lpv(kseq[i])
    for pu, pq in pending_epi:
        emit_epilogue(pu, pq)

    S.op("sync", lambda sy: sy.nop(), deps=out_handles, tick=False)


def build_attn_nc(U=4, L=8192, QC=512, NSLOT=6):
    from contextlib import ExitStack
    nc = bass.Bass("TRN2", target_bir_lowering=False)
    xt = nc.dram_tensor("xt_heads", [U, P, L], F32, kind="ExternalInput").ap()
    xv = nc.dram_tensor("xv_heads", [U, L, P], F32, kind="ExternalInput").ap()
    oo = nc.dram_tensor("o_heads", [U, L, P], F32, kind="ExternalOutput").ap()
    with ExitStack() as ctx:
        S = Sched(nc, ctx)
        block = ctx.enter_context(nc.Block())
        emit_attn(nc, S, ctx, xt, xv, oo, U, L, QC, NSLOT)
        S.flush(block)
    return nc


EPS_TERN = 1e-6
EPS_LN_OUTER = 1e-5
EPS_LN_INNER = 1e-6


def emit_ffn(nc, S, ctx, block, o_rows, x_rows, wfcT, wprojT, y_rows,
             TOK, DM, DF, OB=256, NG=2, EBW=None,
             o_gather=None, gamma_pre=None):
    from contextlib import ExitStack
    e = ctx.enter_context
    ST = TOK // P
    CI = DM // P
    OCH = DF // P
    NOB = DF // OB
    EBW = EBW or min(512, DM)
    NEB = DM // EBW
    STG = ST // NG
    GW = min(DM, DF, 4096)
    NGT = (DM * DF) // (P * GW)
    T2SL = 4
    last = {}

    v_scr = nc.dram_tensor("v_scr", [TOK, DF], BF16).ap()
    t2_scr = nc.dram_tensor("t2_scr", [DF, DM], BF16).ap()
    x1_scr = nc.dram_tensor("x1_scr", [TOK, DM], BF16).ap()

    bar0 = S.barriers()
    identb2 = e(nc.sbuf_tensor("identb2", [P, P], BF16))
    identf2 = e(nc.sbuf_tensor("identf2", [P, P], F32))
    ones_row = e(nc.sbuf_tensor("ones_row", [1, P], F32))
    consts = [_identity(S, identb2[:], deps=bar0), _identity(S, identf2[:], deps=bar0),
              S.op("pool", lambda g: g.memset(ones_row[:], 1.0), deps=bar0)]
    if gamma_pre is None:
        gparam = e(nc.sbuf_tensor("gparam", [P, 6], F32))
    else:
        gparam = gamma_pre[0]
    qparam = e(nc.sbuf_tensor("qparam", [P, 4 * ST], F32))
    t2srow = e(nc.sbuf_tensor("t2srow", [1, DM], BF16))
    junk = e(nc.sbuf_tensor("junk", [P, max(DM, OB)], BF16))
    scol = e(nc.sbuf_tensor("scol", [P, 32], F32))
    from contextlib import ExitStack as _ES
    ctxBC = _ES()
    xhT = ctxBC.enter_context(nc.sbuf_tensor("xhT", [P, CI * TOK], BF16))
    vstats = ctxBC.enter_context(nc.sbuf_tensor("vstats", [P, 2 * ST * NOB], F32))

    def VS(which, st, ob):
        c = which * ST * NOB + st * NOB + ob
        return vstats[:, c:c + 1]

    def QP(which, st):
        c = which * ST + st
        return qparam[:, c:c + 1]

    h_g = [None, None]
    if gamma_pre is not None:
        h_g = list(gamma_pre[1])

    if gamma_pre is None:
      with ExitStack() as ctxA:
        ea = ctxA.enter_context
        gbuf = [ea(nc.sbuf_tensor(f"gbuf{i}", [P, GW], F32)) for i in range(2)]
        gcols = ea(nc.sbuf_tensor("gcols", [P, 2 * NGT], F32))
        gtmp = ea(nc.sbuf_tensor("gtmp", [P, 2], F32))
        grow = ea(nc.sbuf_tensor("grow", [1, P], F32))
        gsc = ea(nc.sbuf_tensor("gsc", [1, 8], F32))
        psga = ea(nc.psum_tensor("psga", [1, P], F32))
        psgb = ea(nc.psum_tensor("psgb", [P, 4], F32))

        hred = {0: [], 1: []}
        for wi, W in ((0, wfcT), (1, wprojT)):
            WT = W.rearrange("(r p) (c w) -> r c p w", p=P, w=GW)
            nr, ncc = W.shape[0] // P, W.shape[1] // GW
            assert nr * ncc == NGT
            for ti in range(nr * ncc):
                slot = ti % 2
                src = WT[ti // ncc, ti % ncc]
                hd = S.dma("sync", lambda sy, s=src, d=gbuf[slot]: sy.dma_start(
                    out=d[:], in_=s), deps=[last.get(("gbuf", slot))],
                    sem=f"g{slot}")
                hr = S.op("dve", lambda v, slot=slot, c=wi * NGT + ti: v.tensor_reduce(
                    gcols[:, c:c + 1], gbuf[slot][:], AX.X, ALU.add,
                    apply_absolute_value=True), deps=[hd])
                last[("gbuf", slot)] = hr
                hred[wi].append(hr)

        for wi in range(2):
            NEL = float(DM * DF)
            h1 = S.op("dve", lambda v, wi=wi: v.tensor_reduce(
                gtmp[:, wi:wi + 1], gcols[:, wi * NGT:(wi + 1) * NGT],
                AX.X, ALU.add), deps=[hred[wi][-1]])
            h2 = S.op("te", lambda t, wi=wi: t.transpose(
                psga[:], gtmp[:, wi:wi + 1], identf2[:]),
                deps=[h1, consts[1], last.get(("psga",))])
            h3 = S.op("dve", lambda v: v.tensor_copy(grow[:], psga[:]),
                      deps=[h2, last.get(("grow",))])
            last[("psga",)] = h3
            h4 = S.op("dve", lambda v, wi=wi: v.tensor_reduce(
                gsc[0:1, wi * 3:wi * 3 + 1], grow[:], AX.X, ALU.add), deps=[h3])
            last[("grow",)] = h4
            h5 = S.op("dve", lambda v, wi=wi, NEL=NEL: v.tensor_scalar_mul(
                gsc[0:1, wi * 3:wi * 3 + 1], gsc[0:1, wi * 3:wi * 3 + 1],
                1.0 / NEL), deps=[h4])
            h6 = S.op("dve", lambda v, wi=wi: v.tensor_scalar(
                gsc[0:1, wi * 3 + 1:wi * 3 + 2], gsc[0:1, wi * 3:wi * 3 + 1],
                EPS_TERN, 0.5, ALU.add, ALU.mult), deps=[h5])
            h7 = S.op("te", lambda t, wi=wi: t.matmul(
                psgb[:, 0:2], ones_row[:], gsc[0:1, wi * 3:wi * 3 + 2],
                start=True, stop=True),
                deps=[h6, consts[2], last.get(("psgb",))])
            h8 = S.op("dve", lambda v, wi=wi: v.tensor_copy(
                gparam[:, wi * 3:wi * 3 + 2], psgb[:, 0:2]), deps=[h7])
            last[("psgb",)] = h8
            h9 = S.op("dve", lambda v, wi=wi: v.tensor_scalar_mul(
                gparam[:, wi * 3 + 2:wi * 3 + 3],
                gparam[:, wi * 3 + 1:wi * 3 + 2], -1.0), deps=[h8])
            h_g[wi] = h9
        S.flush(block)

    def GAM(wi):
        return gparam[:, wi * 3:wi * 3 + 1]

    def TB(wi):
        return gparam[:, wi * 3 + 1:wi * 3 + 2]

    def NTB(wi):
        return gparam[:, wi * 3 + 2:wi * 3 + 3]

    hB_done = []
    h_x1w = {}
    barB = S.barriers()
    with ExitStack() as ctxB:
        eb = ctxB.enter_context
        obuf = ([eb(nc.sbuf_tensor(f"obuf{i}", [P, DM], F32)) for i in range(2)]
                if o_gather is None else [None, None])
        xbuf = [eb(nc.sbuf_tensor(f"xbuf{i}", [P, DM], F32)) for i in range(2)]
        tbuf = [eb(nc.sbuf_tensor(f"tbuf{i}", [P, DM], F32)) for i in range(2)]
        x1b = [eb(nc.sbuf_tensor(f"x1b{i}", [P, DM], BF16)) for i in range(2)]
        ubuf = [eb(nc.sbuf_tensor(f"ubuf{i}", [P, DM], BF16)) for i in range(2)]
        pstr = [eb(nc.psum_tensor(f"pstrB{i}", [P, P], BF16)) for i in range(2)]

        if o_gather is not None:
            obuf_all, h_gath, ctx_ob = o_gather
        for st in range(ST):
            sl = st % 2
            if o_gather is None:
                ho = S.dma("sync", lambda sy, st=st, sl=sl: sy.dma_start(
                    out=obuf[sl][:], in_=o_rows[st * P:(st + 1) * P, :]),
                    deps=[last.get(("obuf", sl)), barB], sem=f"bo{sl}")
            else:
                ho = list(h_gath[st])
            hx = S.dma("sync", lambda sy, st=st, sl=sl: sy.dma_start(
                out=xbuf[sl][:], in_=x_rows[st * P:(st + 1) * P, :]),
                deps=[last.get(("xbuf", sl)), barB], sem=f"bx{sl}")
            osrc = (obuf[sl][:] if o_gather is None
                    else obuf_all[:, st * DM:(st + 1) * DM])
            h_add = S.op("pool", lambda v, sl=sl, osrc=osrc: v.tensor_add(
                tbuf[sl][:], osrc, xbuf[sl][:]),
                deps=[ho, hx, last.get(("tbuf", sl))])
            last[("obuf", sl)] = h_add
            last[("xbuf", sl)] = h_add

            def ln_stats(src, eps, D, c0, dep):
                mcol = scol[:, c0:c0 + 1]
                vcol = scol[:, c0 + 1:c0 + 2]
                rcol = scol[:, c0 + 2:c0 + 3]
                hs1 = S.op("dve", lambda v: v.tensor_reduce(
                    mcol, src, AX.X, ALU.add), deps=[dep, last.get(("scol", c0))])
                hs2 = S.op("act", lambda a: a.activation(
                    out=junk[:, 0:D], in_=src, func=AF.Square, accum_out=vcol),
                    deps=[dep, last.get(("junk",)), last.get(("scol", c0 + 1))])
                last[("junk",)] = hs2
                hm = S.op("dve", lambda v, D=D: v.tensor_scalar_mul(
                    mcol, mcol, 1.0 / D), deps=[hs1])
                hv1 = S.op("dve", lambda v, D=D: v.tensor_scalar_mul(
                    vcol, vcol, 1.0 / D), deps=[hs2])
                hv2 = S.op("dve", lambda v: v.scalar_tensor_tensor(
                    vcol, mcol, mcol, vcol, ALU.mult, ALU.subtract),
                    deps=[hm, hv1])
                hv3 = S.op("dve", lambda v, eps=eps: v.tensor_scalar(
                    vcol, vcol, -1.0, eps, ALU.mult, ALU.add), deps=[hv2])
                hl = S.op("act", lambda s: s.activation(
                    out=rcol, in_=vcol, func=AF.Ln),
                    deps=[hv3, last.get(("scol", c0 + 2))])
                hr = S.op("act", lambda s: s.activation(
                    out=rcol, in_=rcol, func=AF.Exp, scale=-0.5), deps=[hl])
                return hr, mcol, rcol

            hr1, m1, r1 = ln_stats(tbuf[sl][:], EPS_LN_OUTER, DM, 4 * sl, h_add)
            nmr1 = scol[:, 4 * sl + 3:4 * sl + 4]
            h_nmr1 = S.op("dve", lambda v, m1=m1, r1=r1, nmr1=nmr1: v.tensor_scalar(
                nmr1, m1, r1, -1.0, ALU.mult, ALU.mult),
                deps=[hr1, last.get(("scol", 4 * sl + 3))])
            h_x1 = S.op("act", lambda a, sl=sl, r1=r1, nmr1=nmr1: a.activation(
                out=x1b[sl][:], in_=tbuf[sl][:], func=AF.Identity,
                scale=r1, bias=nmr1),
                deps=[h_nmr1, last.get(("x1b", sl))])
            last[("tbuf", sl)] = h_x1
            for c in range(4 * sl, 4 * sl + 4):
                last[("scol", c)] = h_x1
            h_x1w[st] = S.dma("sync", lambda sy, st=st, sl=sl: sy.dma_start(
                out=x1_scr[st * P:(st + 1) * P, :], in_=x1b[sl][:]),
                deps=[h_x1], sem=f"x1w{sl}")

            c2 = 8 + 4 * sl
            qcl = scol[:, c2:c2 + 1]
            rccl = scol[:, c2 + 1:c2 + 2]
            nmrc = scol[:, c2 + 2:c2 + 3]
            vcol1 = scol[:, 4 * sl + 1:4 * sl + 2]
            h_q1 = S.op("dve", lambda v, qcl=qcl, vcol1=vcol1: v.reciprocal(
                qcl, vcol1), deps=[hr1, last.get(("scol", c2))])
            h_q2 = S.op("dve", lambda v, qcl=qcl: v.tensor_scalar(
                qcl, qcl, -EPS_LN_OUTER, 1.0 + EPS_LN_INNER, ALU.mult, ALU.add),
                deps=[h_q1])
            h_q3 = S.op("act", lambda a, qcl=qcl, rccl=rccl: a.activation(
                out=rccl, in_=qcl, func=AF.Ln),
                deps=[h_q2, last.get(("scol", c2 + 1))])
            h_q4 = S.op("act", lambda a, rccl=rccl: a.activation(
                out=rccl, in_=rccl, func=AF.Exp, scale=-0.5), deps=[h_q3])
            h_q5 = S.op("dve", lambda v, rccl=rccl, r1=r1: v.tensor_mul(
                rccl, rccl, r1), deps=[h_q4])
            h_nmrc = S.op("dve", lambda v, m1=m1, rccl=rccl, nmrc=nmrc: v.tensor_scalar(
                nmrc, m1, rccl, -1.0, ALU.mult, ALU.mult),
                deps=[h_q5, last.get(("scol", c2 + 2))])
            h_u = S.op("act", lambda a, sl=sl, rccl=rccl, nmrc=nmrc: a.activation(
                out=ubuf[sl][:], in_=tbuf[sl][:], func=AF.Identity,
                scale=rccl, bias=nmrc),
                deps=[h_nmrc, last.get(("ubuf", sl))])
            for c in (c2, c2 + 1, c2 + 2):
                last[("scol", c)] = h_u
            last[("x1b", sl)] = [h_x1w[st]]
            last[("tbuf", sl)] = [h_x1, h_u]
            h_xh = h_u
            for ci in range(CI):
                tsl = ci % 2
                htr = S.op("te", lambda t, sl=sl, ci=ci, tsl=tsl: t.transpose(
                    pstr[tsl][:], ubuf[sl][:, ci * P:(ci + 1) * P], identb2[:]),
                    deps=[h_xh, consts[0], last.get(("pstrB", tsl))])
                hcp = S.op("act", lambda a, st=st, ci=ci, tsl=tsl: a.copy(
                    xhT[:, ci * TOK + st * P:ci * TOK + (st + 1) * P],
                    pstr[tsl][:]), deps=[htr])
                last[("pstrB", tsl)] = hcp
            last[("ubuf", sl)] = [h_xh, hcp]
            hB_done.append(hcp)
        S.flush(block)
        if o_gather is not None:
            ctx_ob.close()

    h_vw = []
    h_t2w = []
    barC = S.barriers()
    with ExitStack() as ctxC:
        ec = ctxC.enter_context
        NWF = 3
        wf = [ec(nc.sbuf_tensor(f"wf{i}", [P, CI * OB], F32)) for i in range(NWF)]
        t1b = [ec(nc.sbuf_tensor(f"t1b{i}", [P, CI * OB], BF16)) for i in range(3)]
        bbuf = ec(nc.sbuf_tensor("bbuf", [P, max(CI * OB, DM)], BF16))
        bbuf2 = ec(nc.sbuf_tensor("bbuf2", [P, DM], BF16))
        bbuf3 = ec(nc.sbuf_tensor("bbuf3", [P, DM], BF16))
        vstage = [ec(nc.sbuf_tensor(f"vstage{i}", [P, ST * OB], BF16)) for i in range(2)]
        wp = [ec(nc.sbuf_tensor(f"wp{i}", [P, DM], F32)) for i in range(2)]
        ones_c = ec(nc.sbuf_tensor("ones_c", [P, 1], BF16))
        psT2 = [ec(nc.psum_tensor(f"psT2_{i}", [1, min(512, DM)], F32))
                for i in range(max(1, DM // 512))]
        h_onesc = S.op("pool", lambda g: g.memset(ones_c[:], 1.0), deps=barC)
        t2b = [ec(nc.sbuf_tensor(f"t2b{i}", [P, DM], BF16)) for i in range(2)]
        psh = [ec(nc.psum_tensor(f"psh{i}", [P, OB], F32))[:] for i in range(4)]

        wfT = wfcT.rearrange("(ci p) o -> p ci o", p=P)
        wpT = wprojT.rearrange("(och p) e -> och p e", p=P)
        v_scrT = v_scr.rearrange("(st p) o -> p st o", p=P)

        t2_state = {"och": 0}

        def emit_t2(n):
            for _ in range(n):
                och = t2_state["och"]
                if och >= OCH:
                    return
                t2_state["och"] += 1
                sl = och % 2
                hd = S.dma("pool", lambda sy, och=och, sl=sl: sy.dma_start(
                    out=wp[sl][:], in_=wpT[och]),
                    deps=[last.get(("wp", sl)), barC], sem=f"wp{sl}")
                hb = S.op("dve", lambda v, sl=sl: v.tensor_scalar(
                    bbuf3[:, 0:DM], wp[sl][:], NTB(1), None, ALU.is_lt),
                    deps=[hd, h_g[1], last.get(("bbuf3",))])
                ht = S.op("dve", lambda v, sl=sl: v.scalar_tensor_tensor(
                    t2b[sl][:], wp[sl][:], TB(1), bbuf3[:, 0:DM],
                    ALU.is_gt, ALU.subtract),
                    deps=[hb, last.get(("t2b", sl))])
                last[("bbuf3",)] = ht
                last[("wp", sl)] = ht
                last[("bbuf2",)] = ht
                EW = min(512, DM)

                def emsum(t, och=och, sl=sl, EW=EW):
                    for j in range(DM // EW):
                        mm = t.matmul(psT2[j][0:1, :], ones_c[:],
                                      t2b[sl][:, j * EW:(j + 1) * EW],
                                      start=och == 0, stop=och == OCH - 1)
                    return mm
                hts = S.op("te", emsum, deps=[ht, h_onesc])
                t2_state["hsum"] = hts
                hw = S.dma("pool", lambda sy, och=och, sl=sl: sy.dma_start(
                    out=t2_scr[och * P:(och + 1) * P, :], in_=t2b[sl][:]),
                    deps=[ht, hts], sem=f"t2w{sl}")
                last[("t2b", sl)] = hw
                h_t2w.append(hw)

        tern_h = {}

        def emit_tern(ob):
            if ob >= NOB or ob in tern_h:
                return
            sl = ob % NWF
            slw = ob % 3
            hwf = S.dma("sync", lambda sy, ob=ob, sl=sl: sy.dma_start(
                out=wf[sl].ap().rearrange("p (ci o) -> p ci o", o=OB),
                in_=wfT[:, :, ob * OB:(ob + 1) * OB]),
                deps=[last.get(("wf", sl)), barC], sem=f"wf{sl}")
            hb1 = S.op("dve", lambda v, sl=sl: v.tensor_scalar(
                bbuf[:, 0:CI * OB], wf[sl][:], NTB(0), None, ALU.is_lt),
                deps=[hwf, h_g[0], last.get(("bbuf",))])
            ht1 = S.op("dve", lambda v, sl=sl, slw=slw: v.scalar_tensor_tensor(
                t1b[slw][:], wf[sl][:], TB(0), bbuf[:, 0:CI * OB],
                ALU.is_gt, ALU.subtract),
                deps=[hb1, last.get(("t1b", slw))])
            last[("wf", sl)] = ht1
            last[("bbuf",)] = ht1
            tern_h[ob] = ht1

        emit_tern(0)
        emit_tern(1)
        emit_tern(2)
        for ob in range(NOB):
            sl2 = ob % 2
            slw = ob % 3
            ht1 = tern_h[ob]
            lastmm_ob = None

            for st in range(ST):
                hsl = st % 4
                deps = [ht1, hB_done[st], last.get(("psh", hsl))]

                def em(t, st=st, slw=slw, hsl=hsl):
                    for ci in range(CI):
                        mm = t.matmul(psh[hsl],
                                      xhT[:, ci * TOK + st * P:ci * TOK + (st + 1) * P],
                                      t1b[slw][:, ci * OB:(ci + 1) * OB],
                                      start=ci == 0, stop=ci == CI - 1)
                    return mm
                hmm = S.op("te", em, deps=deps)
                lastmm_ob = hmm
                hge = S.op("act", lambda s, st=st, ob=ob, sl2=sl2, hsl=hsl: s.activation(
                    out=vstage[sl2][:, st * OB:(st + 1) * OB], in_=psh[hsl],
                    func=AF.Gelu, scale=GAM(0), accum_out=VS(0, st, ob)),
                    deps=[hmm, last.get(("vstage", sl2)), h_g[0]])
                last[("psh", hsl)] = hge
                hsq = S.op("act", lambda a, st=st, ob=ob, sl2=sl2: a.activation(
                    out=junk[:, 0:OB], in_=vstage[sl2][:, st * OB:(st + 1) * OB],
                    func=AF.Square, accum_out=VS(1, st, ob)),
                    deps=[hge, last.get(("junk",))])
                last[("junk",)] = hsq
                last.setdefault(("vst_done", sl2), [])
                if st == 0:
                    last[("vst_done", sl2)] = []
                last[("vst_done", sl2)] += [hge, hsq]
            last[("t1b", slw)] = lastmm_ob
            emit_tern(ob + 3)
            hvw = S.dma("pool", lambda sy, ob=ob, sl2=sl2: sy.dma_start(
                out=v_scrT[:, :, ob * OB:(ob + 1) * OB],
                in_=vstage[sl2].ap().rearrange("p (st o) -> p st o", o=OB)),
                deps=list(last[("vst_done", sl2)]), sem=f"vw{sl2}")
            last[("vstage", sl2)] = hvw
            h_vw.append(hvw)
            emit_t2(max(1, OCH // NOB + 1))
        emit_t2(OCH)
        EW = min(512, DM)
        h_t2row = []
        for j in range(DM // EW):
            h_t2row.append(S.op("dve", lambda v, j=j, EW=EW: v.tensor_copy(
                t2srow[0:1, j * EW:(j + 1) * EW], psT2[j][0:1, :]),
                deps=[t2_state["hsum"]]))
        S.flush(block)

    for st in range(ST):
        hs1 = S.op("dve", lambda v, st=st: v.tensor_reduce(
            QP(0, st), vstats[:, 0 * ST * NOB + st * NOB:0 * ST * NOB + (st + 1) * NOB],
            AX.X, ALU.add), deps=h_vw[-2:])
        hs2 = S.op("dve", lambda v, st=st: v.tensor_reduce(
            QP(1, st), vstats[:, 1 * ST * NOB + st * NOB:1 * ST * NOB + (st + 1) * NOB],
            AX.X, ALU.add), deps=h_vw[-2:])
        hm = S.op("dve", lambda v, st=st: v.tensor_scalar_mul(
            QP(0, st), QP(0, st), 1.0 / DF), deps=[hs1])
        hv1 = S.op("dve", lambda v, st=st: v.tensor_scalar_mul(
            QP(1, st), QP(1, st), 1.0 / DF), deps=[hs2])
        hv2 = S.op("dve", lambda v, st=st: v.scalar_tensor_tensor(
            QP(1, st), QP(0, st), QP(0, st), QP(1, st), ALU.mult, ALU.subtract),
            deps=[hm, hv1])
        hv3 = S.op("dve", lambda v, st=st: v.tensor_scalar(
            QP(1, st), QP(1, st), -1.0, EPS_LN_INNER, ALU.mult, ALU.add),
            deps=[hv2])
        hl = S.op("act", lambda s, st=st: s.activation(
            out=QP(1, st), in_=QP(1, st), func=AF.Ln), deps=[hv3])
        hr = S.op("act", lambda s, st=st: s.activation(
            out=QP(1, st), in_=QP(1, st), func=AF.Exp, scale=-0.5), deps=[hl])
        hg2 = S.op("dve", lambda v, st=st: v.tensor_mul(
            QP(2, st), QP(1, st), GAM(1)), deps=[hr, h_g[1]])
        hg3 = S.op("dve", lambda v, st=st: v.tensor_scalar(
            QP(3, st), QP(0, st), QP(2, st), -1.0, ALU.mult, ALU.mult),
            deps=[hg2, hm])
        last[("qp", st)] = hg3

    ctxBC.close()

    h_out = []
    vhT = e(nc.sbuf_tensor("vhT", [P, OCH * STG * P], BF16))
    t2p = [e(nc.sbuf_tensor(f"t2p{i}", [P, EBW], BF16)) for i in range(T2SL)]
    x2pre = e(nc.sbuf_tensor("x2pre", [P, STG * DM], BF16))
    x1t = [e(nc.sbuf_tensor(f"x1t{i}", [P, DM], BF16)) for i in range(2)]
    tt2 = e(nc.sbuf_tensor("tt2", [P, DM], F32))
    outst = [e(nc.sbuf_tensor("outst0", [P, DM], F32))]

    h_vw_all = (S.dma_sems.get("vw0") and None) or None
    def grand(sem_prefix, nslots=2):
        hs = []
        for i in range(nslots):
            nm = f"{sem_prefix}{i}"
            if nm in S.dma_sems:
                hs.append((S.dma_sems[nm], S.dma_cnt[nm]))
        return hs

    h_v_all = grand("vw")
    h_t2_all = grand("t2w")
    h_x1_all = grand("x1w")
    barD = S.barriers()
    h_v_all += barD
    h_t2_all += barD
    h_x1_all += barD

    T2B = e(nc.sbuf_tensor("T2B", [P, DM], F32))
    with _ES() as ctxTB:
        pbc = ctxTB.enter_context(nc.psum_tensor("pbcD", [P, min(512, DM)], F32))
        onesrD = ctxTB.enter_context(nc.sbuf_tensor("onesrD", [1, P], BF16))
        h_or = S.op("pool", lambda g: g.memset(onesrD[:], 1.0), deps=barD)
        EW = min(512, DM)
        for j in range(DM // EW):
            hbm = S.op("te", lambda t, j=j, EW=EW: t.matmul(
                pbc[:], onesrD[:], t2srow[0:1, j * EW:(j + 1) * EW],
                start=True, stop=True),
                deps=[h_or, last.get(("pbcD",))] + h_t2row + barD)
            hbc = S.op("dve", lambda v, j=j, EW=EW: v.tensor_copy(
                T2B[:, j * EW:(j + 1) * EW], pbc[:]), deps=[hbm])
            last[("pbcD",)] = hbc
        last[("T2B",)] = hbc
        S.flush(block)

    psy = [e(nc.psum_tensor(f"psy{i}", [P, EBW], F32)) for i in range(STG)]
    for g in range(NG):
        trs = []
        for och in range(OCH):
            trs.append(S.dma("sync", lambda sy, g=g, och=och: sy.dma_start_transpose(
                out=vhT[:, och * STG * P:(och + 1) * STG * P],
                in_=v_scr[g * STG * P:(g + 1) * STG * P,
                          och * P:(och + 1) * P]),
                deps=h_v_all + [last.get(("vhT", och))], sem=f"vt{och % 4}"))
        bar_build = trs

        if True:
            for eb in range(NEB):
                for och in range(OCH):
                    tsl = (eb * OCH + och) % T2SL
                    hd = S.dma("sync", lambda sy, eb=eb, och=och, tsl=tsl: sy.dma_start(
                        out=t2p[tsl][:],
                        in_=t2_scr[och * P:(och + 1) * P, eb * EBW:(eb + 1) * EBW]),
                        deps=h_t2_all + [last.get(("t2p", tsl))], sem=f"t2r{tsl}")

                    def em(t, eb=eb, och=och, tsl=tsl, g=g):
                        for st8 in range(STG):
                            mm = t.matmul(
                                psy[st8][:],
                                vhT[:, och * STG * P + st8 * P:och * STG * P + (st8 + 1) * P],
                                t2p[tsl][:], start=och == 0, stop=och == OCH - 1)
                        return mm
                    deps = [hd, trs[och]]
                    if och == 0:
                        deps += [last.get(("psy", st8)) for st8 in range(STG)]
                        deps.append(last.get(("T2B",)))
                    hmm = S.op("te", em, deps=deps)
                    last[("t2p", tsl)] = hmm
                    if eb == NEB - 1:
                        last[("vhT", och)] = hmm
                for st8 in range(STG):
                    st = g * STG + st8
                    hev = S.op("dve", lambda v, st8=st8, eb=eb, st=st: v.tensor_scalar_mul(
                        x2pre[:, st8 * DM + eb * EBW:st8 * DM + (eb + 1) * EBW],
                        psy[st8][:], QP(2, st)),
                        deps=[hmm, last.get(("qp", st)), last.get(("x2pre", st8, eb))])
                    last[("psy", st8)] = hev
                    last[("evac", g, eb, st8)] = hev
            for st8 in range(STG):
                st = g * STG + st8
                sl = st % 2
                hx1 = S.dma("sync", lambda sy, st=st, sl=sl: sy.dma_start(
                    out=x1t[sl][:], in_=x1_scr[st * P:(st + 1) * P, :]),
                    deps=h_x1_all + [last.get(("x1t", sl))], sem=f"x1r{sl}")
                evdeps = [last[("evac", g, eb, st8)] for eb in range(NEB)]
                hadd0 = S.op("dve", lambda v, st=st, sl=sl: v.scalar_tensor_tensor(
                    tt2[:], T2B[:], QP(3, st), x1t[sl][:], ALU.mult, ALU.add),
                    deps=[hx1, last.get(("qp", st)), last.get(("tt2",))] + (
                        [last.get(("T2B",))] if last.get(("T2B",)) else []))
                hadd = S.op("dve", lambda v, st8=st8: v.tensor_add(
                    tt2[:], tt2[:], x2pre[:, st8 * DM:(st8 + 1) * DM]),
                    deps=[hadd0] + evdeps)
                last[("x1t", sl)] = hadd
                mcol = scol[:, 20:21]
                vcol = scol[:, 21:22]
                rcol = scol[:, 22:23]
                hs1 = S.op("dve", lambda v: v.tensor_reduce(
                    tt2[:, 0:1] if False else mcol, tt2[:], AX.X, ALU.add),
                    deps=[hadd, last.get(("scolD",))])
                hs2 = S.op("dve", lambda v: v.scalar_tensor_tensor(
                    junk[:, 0:DM], tt2[:], 1.0, tt2[:], ALU.mult, ALU.mult,
                    accum_out=vcol), deps=[hadd, last.get(("junk",))])
                last[("junk",)] = hs2
                hm = S.op("dve", lambda v: v.tensor_scalar_mul(
                    mcol, mcol, 1.0 / DM), deps=[hs1])
                hv1 = S.op("dve", lambda v: v.tensor_scalar_mul(
                    vcol, vcol, 1.0 / DM), deps=[hs2])
                hv2 = S.op("dve", lambda v: v.scalar_tensor_tensor(
                    vcol, mcol, mcol, vcol, ALU.mult, ALU.subtract),
                    deps=[hm, hv1])
                hv3 = S.op("dve", lambda v: v.tensor_scalar(
                    vcol, vcol, -1.0, EPS_LN_OUTER, ALU.mult, ALU.add),
                    deps=[hv2])
                hl = S.op("act", lambda s: s.activation(
                    out=rcol, in_=vcol, func=AF.Ln), deps=[hv3])
                hr = S.op("act", lambda s: s.activation(
                    out=rcol, in_=rcol, func=AF.Exp, scale=-0.5), deps=[hl])
                osl = 0
                happ = S.op("dve", lambda v, osl=osl: v.tensor_scalar(
                    outst[osl][:], tt2[:], mcol, rcol, ALU.subtract, ALU.mult),
                    deps=[hr, last.get(("outst", osl))])
                last[("tt2",)] = happ
                last[("scolD",)] = happ
                ho = S.dma("sync", lambda sy, st=st, osl=osl: sy.dma_start(
                    out=y_rows[st * P:(st + 1) * P, :], in_=outst[osl][:]),
                    deps=[happ], sem=f"yw{osl}")
                last[("outst", osl)] = ho
                last[("x2pre_free", st8)] = happ
                for eb in range(NEB):
                    last[("x2pre", st8, eb)] = happ
                h_out.append(ho)
            S.flush(block)

    S.op("sync", lambda sy: sy.nop(), deps=h_out, tick=False)
    S.flush(block)


def build_ffn_nc(TOK=2048, DM=2048, DF=8192, OB=256, NG=2, EBW=None):
    from contextlib import ExitStack
    nc = bass.Bass("TRN2", target_bir_lowering=False)
    o_rows = nc.dram_tensor("o_rows", [TOK, DM], F32, kind="ExternalInput").ap()
    x_rows = nc.dram_tensor("x_rows", [TOK, DM], F32, kind="ExternalInput").ap()
    wfcT = nc.dram_tensor("wfcT", [DM, DF], F32, kind="ExternalInput").ap()
    wprojT = nc.dram_tensor("wprojT", [DF, DM], F32, kind="ExternalInput").ap()
    y_rows = nc.dram_tensor("y_rows", [TOK, DM], F32, kind="ExternalOutput").ap()
    with ExitStack() as ctx:
        S = Sched(nc, ctx)
        block = ctx.enter_context(nc.Block())
        emit_ffn(nc, S, ctx, block, o_rows, x_rows, wfcT, wprojT, y_rows,
                 TOK, DM, DF, OB, NG, EBW)
    return nc


NCORE = 8
B_, L_, DM_, DF_, H_ = 2, 8192, 2048, 8192, 16
UPC = (B_ * H_) // NCORE
TPC = (B_ * L_) // NCORE


def kernel(L0_attn_x, L0_ffn_w_fc, L0_ffn_w_proj):
    return kernel_fused(L0_attn_x, L0_ffn_w_fc, L0_ffn_w_proj)


def kernel_v0(L0_attn_x, L0_ffn_w_fc, L0_ffn_w_proj):
    x = np.ascontiguousarray(np.asarray(L0_attn_x, dtype=np.float32))
    wfcT = np.ascontiguousarray(np.asarray(L0_ffn_w_fc, np.float32).T)
    wprojT = np.ascontiguousarray(np.asarray(L0_ffn_w_proj, np.float32).T)

    nc1 = build_attn_nc(U=UPC, L=L_, QC=512, NSLOT=8)
    in1 = []
    for c in range(NCORE):
        xts, xvs = [], []
        for j in range(UPC):
            u = UPC * c + j
            b, h = divmod(u, H_)
            sl = x[b, :, h * P:(h + 1) * P]
            xvs.append(sl)
            xts.append(sl.T)
        in1.append({"xt_heads": np.ascontiguousarray(np.stack(xts)),
                    "xv_heads": np.ascontiguousarray(np.stack(xvs))})
    r1 = run_bass_kernel_spmd(nc1, in1, core_ids=list(range(NCORE)))

    o_flat = np.empty((B_ * L_, DM_), np.float32)
    for c in range(NCORE):
        oh = r1.results[c]["o_heads"]
        for j in range(UPC):
            u = UPC * c + j
            b, h = divmod(u, H_)
            o_flat[b * L_:(b + 1) * L_, h * P:(h + 1) * P] = oh[j]

    x_flat = x.reshape(B_ * L_, DM_)
    nc2 = build_ffn_nc(TOK=TPC, DM=DM_, DF=DF_, OB=256, NG=2, EBW=512)
    in2 = [{"o_rows": np.ascontiguousarray(o_flat[c * TPC:(c + 1) * TPC]),
            "x_rows": np.ascontiguousarray(x_flat[c * TPC:(c + 1) * TPC]),
            "wfcT": wfcT, "wprojT": wprojT} for c in range(NCORE)]
    r2 = run_bass_kernel_spmd(nc2, in2, core_ids=list(range(NCORE)))

    y = np.concatenate([r2.results[c]["y_rows"] for c in range(NCORE)], axis=0)
    return y.reshape(B_, L_, DM_)


def emit_gamma(nc, S, ctx, wfcT, wprojT, gparam, ones_row, identf_ap,
               DM, DF, id_dep):
    e = ctx.enter_context
    GW = min(DM, DF, 4096)
    NGT = (DM * DF) // (P * GW)
    gbuf = [e(nc.sbuf_tensor(f"gbufF{i}", [P, GW], F32)) for i in range(2)]
    gcols = e(nc.sbuf_tensor("gcolsF", [P, 2 * NGT], F32))
    gtmp = e(nc.sbuf_tensor("gtmpF", [P, 2], F32))
    grow = e(nc.sbuf_tensor("growF", [1, P], F32))
    gsc = e(nc.sbuf_tensor("gscF", [1, 8], F32))
    last = {}
    hred = {0: [], 1: []}
    tile_fns = []

    for wi, W in ((0, wfcT), (1, wprojT)):
        WT = W.rearrange("(r p) (c w) -> r c p w", p=P, w=GW)
        nr, ncc = W.shape[0] // P, W.shape[1] // GW
        assert nr * ncc == NGT
        for ti in range(nr * ncc):
            def tile(wi=wi, ti=ti, WT=WT, ncc=ncc):
                slot = (len(hred[0]) + len(hred[1])) % 2
                src = WT[ti // ncc, ti % ncc]
                hd = S.dma("sync", lambda sy, src=src, slot=slot: sy.dma_start(
                    out=gbuf[slot][:], in_=src),
                    deps=[last.get(("gbuf", slot))], sem=f"gF{slot}")
                hr = S.op("dve", lambda v, slot=slot, c=wi * NGT + ti: v.tensor_reduce(
                    gcols[:, c:c + 1], gbuf[slot][:], AX.X, ALU.add,
                    apply_absolute_value=True), deps=[hd])
                last[("gbuf", slot)] = hr
                hred[wi].append(hr)
            tile_fns.append(tile)

    h_g = [None, None]

    def finalize(extra_deps=()):
        psga = e(nc.psum_tensor("psgaF", [1, P], F32))
        psgb = e(nc.psum_tensor("psgbF", [P, 4], F32))
        for wi in range(2):
            NEL = float(DM * DF)
            h1 = S.op("dve", lambda v, wi=wi: v.tensor_reduce(
                gtmp[:, wi:wi + 1], gcols[:, wi * NGT:(wi + 1) * NGT],
                AX.X, ALU.add), deps=[hred[wi][-1]] + list(extra_deps))
            h2 = S.op("te", lambda t, wi=wi: t.transpose(
                psga[:], gtmp[:, wi:wi + 1], identf_ap),
                deps=[h1, id_dep, last.get(("psga",))] + list(extra_deps))
            h3 = S.op("dve", lambda v: v.tensor_copy(grow[:], psga[:]),
                      deps=[h2, last.get(("grow",))])
            last[("psga",)] = h3
            h4 = S.op("dve", lambda v, wi=wi: v.tensor_reduce(
                gsc[0:1, wi * 3:wi * 3 + 1], grow[:], AX.X, ALU.add), deps=[h3])
            last[("grow",)] = h4
            h5 = S.op("dve", lambda v, wi=wi, NEL=NEL: v.tensor_scalar_mul(
                gsc[0:1, wi * 3:wi * 3 + 1], gsc[0:1, wi * 3:wi * 3 + 1],
                1.0 / NEL), deps=[h4])
            h6 = S.op("dve", lambda v, wi=wi: v.tensor_scalar(
                gsc[0:1, wi * 3 + 1:wi * 3 + 2], gsc[0:1, wi * 3:wi * 3 + 1],
                EPS_TERN, 0.5, ALU.add, ALU.mult), deps=[h5])
            h7 = S.op("te", lambda t, wi=wi: t.matmul(
                psgb[:, 0:2], ones_row[:], gsc[0:1, wi * 3:wi * 3 + 2],
                start=True, stop=True),
                deps=[h6, last.get(("psgb",))])
            h8 = S.op("dve", lambda v, wi=wi: v.tensor_copy(
                gparam[:, wi * 3:wi * 3 + 2], psgb[:, 0:2]), deps=[h7])
            last[("psgb",)] = h8
            h9 = S.op("dve", lambda v, wi=wi: v.tensor_scalar_mul(
                gparam[:, wi * 3 + 2:wi * 3 + 3],
                gparam[:, wi * 3 + 1:wi * 3 + 2], -1.0), deps=[h8])
            h_g[wi] = h9

    return tile_fns, finalize, h_g


def build_fused_nc(U=4, L=8192, QC=512, NSLOT=8, TOK=2048, DM=2048, DF=8192,
                   OB=256, NG=2, EBW=512):
    from contextlib import ExitStack
    nc = bass.Bass("TRN2", target_bir_lowering=False)
    xt = nc.dram_tensor("xt_heads", [U, P, L], F32, kind="ExternalInput").ap()
    xv = nc.dram_tensor("xv_heads", [U, L, P], F32, kind="ExternalInput").ap()
    x_rows = nc.dram_tensor("x_rows", [TOK, DM], F32, kind="ExternalInput").ap()
    wfcT = nc.dram_tensor("wfcT", [DM, DF], F32, kind="ExternalInput").ap()
    wprojT = nc.dram_tensor("wprojT", [DF, DM], F32, kind="ExternalInput").ap()
    oidx = nc.dram_tensor("oidx", [P, 64], I32, kind="ExternalInput").ap()
    y_rows = nc.dram_tensor("y_rows", [TOK, DM], F32, kind="ExternalOutput").ap()
    o_cc = nc.dram_tensor("o_cc", [U * L, P], BF16).ap()
    o_g = [nc.dram_tensor(f"o_g{u}", [8 * L, P], BF16, addr_space="Shared").ap()
           for u in range(U)]

    ST = TOK // P
    TPB = ST // 2
    NIC = 8 * TPB
    NH = DM // P
    with ExitStack() as ctx:
        S = Sched(nc, ctx)
        block = ctx.enter_context(nc.Block())
        e = ctx.enter_context

        identfF = e(nc.sbuf_tensor("identfF", [P, P], F32))
        ones_rowF = e(nc.sbuf_tensor("ones_rowF", [1, P], F32))
        gparamF = e(nc.sbuf_tensor("gparamF", [P, 6], F32))
        ctx_ob = ExitStack()
        obuf_all = ctx_ob.enter_context(
            nc.sbuf_tensor("obuf_all", [P, ST * DM], BF16, side="right"))
        oidx_sb = e(nc.sbuf_tensor("oidx_sb", [P, NIC], I32))
        h_idf = _identity(S, identfF[:])
        h_or = S.op("pool", lambda g: g.memset(ones_rowF[:], 1.0))
        h_oidx = S.dma("sync", lambda sy: sy.dma_start(
            out=oidx_sb[:], in_=oidx[:, 0:NIC]), sem="oidx")

        h_cc = []
        h_gath = []
        h_gath_st = {}

        with ExitStack() as ctx_gamma:
            tile_fns, gamma_finalize, h_g = emit_gamma(
                nc, S, ctx_gamma, wfcT, wprojT, gparamF, ones_rowF,
                identfF[:], DM, DF, h_idf)
            tile_iter = iter(tile_fns)

            def bg_hook():
                fn = next(tile_iter, None)
                if fn is not None:
                    fn()

            def unit_end_cb(u, outs):
                h = S.op("pool", lambda g, u=u: g.collective_compute(
                    "AllGather", ALU.bypass,
                    replica_groups=[list(range(8))],
                    ins=[o_cc[u * L:(u + 1) * L, :].opt()],
                    outs=[o_g[u].opt()]), deps=outs)
                h_cc.append(h)
                prevg = None
                for st in range(ST):
                    b = st // TPB
                    for hh in range(NH):
                        g_u = b * NH + hh
                        if g_u % U != u:
                            continue
                        r = g_u // U
                        icol = r * TPB + (st % TPB)
                        prevg = S.dma("pool", lambda g, u=u, st=st, hh=hh,
                                      icol=icol: g.indirect_dma_start(
                            out=obuf_all[:, st * DM + hh * P:st * DM + (hh + 1) * P],
                            out_offset=None,
                            in_=o_g[u][:, :],
                            in_offset=IndirectOffsetOnAxis(
                                ap=oidx_sb[:, icol:icol + 1], axis=0)),
                            deps=[h, h_oidx, prevg], sem=f"og{u}")
                        h_gath.append(prevg)
                    h_gath_st.setdefault(st, []).append(prevg)

            with ExitStack() as ctx_attn:
                emit_attn(nc, S, ctx_attn, xt, xv, o_cc, U, L, QC, NSLOT,
                          o_bf16=True, unit_sems=True, bg_hook=bg_hook,
                          unit_end_cb=unit_end_cb)
                for fn in tile_iter:
                    fn()
                S.flush(block)
            gamma_finalize(extra_deps=S.barriers())
            S.flush(block)

        emit_ffn(nc, S, ctx, block, None, x_rows, wfcT, wprojT, y_rows,
                 TOK, DM, DF, OB, NG, EBW,
                 o_gather=(obuf_all, h_gath_st, ctx_ob),
                 gamma_pre=(gparamF, h_g))
    return nc


def kernel_fused(L0_attn_x, L0_ffn_w_fc, L0_ffn_w_proj):
    x = np.ascontiguousarray(np.asarray(L0_attn_x, dtype=np.float32))
    wfcT = np.ascontiguousarray(np.asarray(L0_ffn_w_fc, np.float32).T)
    wprojT = np.ascontiguousarray(np.asarray(L0_ffn_w_proj, np.float32).T)

    nc = build_fused_nc(U=UPC, L=L_, QC=512, NSLOT=8, TOK=TPC, DM=DM_, DF=DF_,
                        OB=256, NG=2, EBW=512)
    SPC = L_ // NCORE
    in_maps = []
    for c in range(NCORE):
        xts, xvs = [], []
        for j in range(UPC):
            u = UPC * c + j
            b, h = divmod(u, H_)
            sl = x[b, :, h * P:(h + 1) * P]
            xvs.append(sl)
            xts.append(sl.T)
        xr = np.concatenate([x[0, c * SPC:(c + 1) * SPC, :],
                             x[1, c * SPC:(c + 1) * SPC, :]], axis=0)
        oidx = np.empty((P, 64), np.int32)
        pp = np.arange(P, dtype=np.int32)
        for r in range(8):
            for srow in range(8):
                oidx[:, r * 8 + srow] = r * L_ + c * SPC + srow * P + pp
        in_maps.append({"xt_heads": np.ascontiguousarray(np.stack(xts)),
                        "xv_heads": np.ascontiguousarray(np.stack(xvs)),
                        "x_rows": np.ascontiguousarray(xr),
                        "wfcT": wfcT, "wprojT": wprojT, "oidx": oidx})
    res = run_bass_kernel_spmd(nc, in_maps, core_ids=list(range(NCORE)))
    out = np.empty((B_, L_, DM_), np.float32)
    for c in range(NCORE):
        y = res.results[c]["y_rows"]
        out[0, c * SPC:(c + 1) * SPC, :] = y[:SPC]
        out[1, c * SPC:(c + 1) * SPC, :] = y[SPC:]
    return out


# version 17
# speedup vs baseline: 1.0932x; 1.0932x over previous
import math
import numpy as np
import concourse.bass as bass
import concourse.mybir as mybir
from concourse.bass_utils import run_bass_kernel_spmd
from concourse.bass import IndirectOffsetOnAxis

F32 = mybir.dt.float32
BF16 = mybir.dt.bfloat16
I32 = mybir.dt.int32
AF = mybir.ActivationFunctionType
ALU = mybir.AluOpType
AX = mybir.AxisListType

P = 128
NEG = -30000.0


class Sched:

    ENGINES = ("te", "act", "dve", "pool", "sync")

    def __init__(self, nc, ctx):
        self.nc = nc
        self.ctx = ctx
        self.pend = {e: [] for e in self.ENGINES}
        self.sems = {e: ctx.enter_context(nc.semaphore(f"tl_{e}")) for e in self.ENGINES}
        self.own = set(s.num for s in self.sems.values())
        self.cnt = {e: 0 for e in self.ENGINES}
        self.dma_sems = {}
        self.dma_cnt = {}
        self.waited = {e: {} for e in self.ENGINES}

    def dma_sem(self, name):
        if name not in self.dma_sems:
            self.dma_sems[name] = self.ctx.enter_context(self.nc.semaphore("ds_" + name))
            self.dma_cnt[name] = 0
        return self.dma_sems[name]

    @staticmethod
    def _flat(deps):
        out = []
        for d in deps:
            if d is None:
                continue
            if isinstance(d, list):
                out.extend(x for x in d if x is not None)
            else:
                out.append(d)
        return out

    def op(self, eng, emit, deps=(), tick=True):
        deps = self._flat(deps)
        h = None
        if tick:
            self.cnt[eng] += 1
            h = (self.sems[eng], self.cnt[eng])
        self.pend[eng].append((deps, emit, h))
        return h

    def dma(self, eng, emit, deps=(), sem="dma"):
        deps = self._flat(deps)
        s = self.dma_sem(sem)
        self.dma_cnt[sem] += 16
        h = (s, self.dma_cnt[sem])
        self.pend[eng].append((deps, emit, h))
        return h

    def barriers(self):
        hs = [(self.sems[e], self.cnt[e]) for e in self.ENGINES if self.cnt[e]]
        hs += [(self.dma_sems[n], self.dma_cnt[n]) for n in self.dma_sems]
        return hs

    def flush(self, block):
        for eng in self.ENGINES:
            ops = self.pend[eng]
            if not ops:
                continue
            self.pend[eng] = []
            dec = {"te": block.tensor, "act": block.scalar, "dve": block.vector,
                   "pool": block.gpsimd, "sync": block.sync}[eng]
            waited = self.waited[eng]
            own = self.own

            def body(E, ops=ops, waited=waited, own=own):
                for deps, emit, h in ops:
                    bysem = {}
                    for (sem, val) in deps:
                        if bysem.get(sem.num, (None, -1))[1] < val:
                            bysem[sem.num] = (sem, val)
                    for key, (sem, val) in sorted(bysem.items()):
                        if waited.get(key, -1) >= val:
                            continue
                        E.wait_ge(sem, val)
                        waited[key] = val
                    inst = emit(E)
                    if h is not None:
                        sem, _ = h
                        inst.then_inc(sem, 1 if sem.num in own else 16)
            dec(body)


def _identity(S, ap, deps=()):
    h0 = S.op("pool", lambda g, ap=ap: g.memset(ap, 0.0), deps=deps)
    return S.op("pool", lambda g, ap=ap: g.affine_select(
        out=ap, in_=ap, compare_op=ALU.not_equal, fill=1.0,
        base=0, pattern=[[-1, ap.shape[1]]], channel_multiplier=1), deps=[h0])


def emit_attn(nc, S, ctx, xt, xv, oo, U, L, QC, NSLOT=8, o_bf16=False,
              unit_sems=False, bg_hook=None, unit_end_cb=None):
    KPQ = QC // P
    NQC = L // QC
    OFF = P * (KPQ - 1)
    SCALE = 1.0 / math.sqrt(128.0)
    NVD = max(1, L // 1024)
    e = ctx.enter_context

    kt = [e(nc.sbuf_tensor(f"kt{i}", [P, L], BF16)) for i in range(2)]
    vt = [e(nc.sbuf_tensor(f"vt{i}", [P, L], BF16)) for i in range(2)]
    pring = e(nc.sbuf_tensor("pring", [P, NSLOT * QC], BF16))
    maskM = e(nc.sbuf_tensor("maskM", [P, OFF + QC], BF16))
    identb = e(nc.sbuf_tensor("identb", [P, P], BF16))
    identff = e(nc.sbuf_tensor("identff", [P, P], F32))
    identf1 = e(nc.sbuf_tensor("identf1", [1, 1], F32))
    onesb = e(nc.sbuf_tensor("onesb", [P, 1], BF16))
    lrow = [e(nc.sbuf_tensor(f"lrow{i}", [1, QC], F32)) for i in range(2)]
    recipT = [e(nc.sbuf_tensor(f"recipT{i}", [P, KPQ], F32)) for i in range(2)]
    osb = [e(nc.sbuf_tensor(f"osb{i}", [P, QC], F32)) for i in range(2)]
    osta = [e(nc.sbuf_tensor(f"osta{i}", [P, KPQ * P], BF16 if o_bf16 else F32))
            for i in range(2)]

    NSB = 4
    pss = [e(nc.psum_tensor(f"pss{i}", [P, QC], F32))[:] for i in range(NSB)]
    psoT = [e(nc.psum_tensor(f"pso{i}", [P, QC], F32)) for i in range(2)]
    psl = e(nc.psum_tensor("psl", [1, QC], F32))
    pstb_t = e(nc.psum_tensor("pstb", [P, KPQ * P], F32))
    pstb = pstb_t[:, 0:KPQ * P]
    pslt = pstb_t[:, 0:KPQ]

    consts = [
        _identity(S, identb[:]),
        _identity(S, identff[:]),
        S.op("pool", lambda g: g.memset(onesb[:], 1.0)),
        S.op("pool", lambda g: g.memset(identf1[:], 1.0)),
    ]
    h_m0 = S.op("pool", lambda g: g.memset(maskM[:], 0.0))
    consts.append(S.op("pool", lambda g: g.affine_select(
        out=maskM[:], in_=maskM[:], compare_op=ALU.is_ge, fill=NEG,
        base=-OFF, pattern=[[1, OFF + QC]], channel_multiplier=-1), deps=[h_m0]))

    def kbs(qc):
        return KPQ * (qc + 1)

    h_kv = {}
    unit_done = {}

    def load_unit(u):
        deps = [unit_done[u - 2]] if u >= 2 else []
        hs = [S.dma("pool", lambda g, u=u: g.dma_start(
            out=kt[u % 2][:], in_=xt[u]), deps=deps, sem=f"kv{u % 2}")]
        vsrc = xv[u].rearrange("(nd kb p) d -> nd p kb d", p=P, nd=NVD)
        LG = L // NVD
        for i in range(NVD):
            vdst = vt[u % 2][:, i * LG:(i + 1) * LG].rearrange(
                "p (kb d) -> p kb d", d=P)
            hs.append(S.dma("pool", lambda g, s=vsrc[i], d=vdst: g.dma_start(
                out=d, in_=s), deps=deps, sem=f"kv{u % 2}"))
        h_kv[u] = hs

    load_unit(0)
    if U > 1:
        load_unit(1)

    kseq = [(u, qc, kb) for u in range(U) for qc in range(NQC)
            for kb in range(kbs(qc))]
    kidx = {key: i for i, key in enumerate(kseq)}
    chunks = [(u, qc) for u in range(U) for qc in range(NQC)]
    cidx = {c: i for i, c in enumerate(chunks)}

    smm, expd, pvd = {}, {}, {}
    evac = {}
    last = {}

    def emit_scores(key):
        u, qc, kb = key
        idx = kidx[key]
        sslot = idx % NSB
        deps = list(consts) + h_kv[u]
        if idx >= NSB:
            deps.append(expd[kseq[idx - NSB]])
        diag = kb >= KPQ * qc
        K = kt[u % 2]

        def em(t, K=K, sslot=sslot, kb=kb, qc=qc, diag=diag):
            mm = t.matmul(pss[sslot], K[:, kb * P:(kb + 1) * P],
                          K[:, qc * QC:(qc + 1) * QC], start=True, stop=not diag)
            if diag:
                w0 = OFF - P * (kb - KPQ * qc)
                mm = t.matmul(pss[sslot], identb[:],
                              maskM[:, w0:w0 + QC], start=False, stop=True)
            return mm
        smm[key] = S.op("te", em, deps=deps)

    def emit_exp(key):
        idx = kidx[key]
        sslot = idx % NSB
        slot = idx % NSLOT
        deps = [smm[key]]
        if idx >= NSLOT:
            deps.append(pvd[kseq[idx - NSLOT]])

        def em(s, sslot=sslot, slot=slot):
            return s.activation(out=pring[:, slot * QC:(slot + 1) * QC],
                                in_=pss[sslot], func=AF.Exp, scale=SCALE)
        expd[key] = S.op("act", em, deps=deps)

    def emit_lpv(key):
        u, qc, kb = key
        idx = kidx[key]
        slot = idx % NSLOT
        deps = [expd[key]]
        if kb == 0:
            deps.append(last.get(("psl",)))
            deps.append(last.get(("pso", qc % 2)))
        lastkb = kb == kbs(qc) - 1

        def em(t, u=u, qc=qc, kb=kb, slot=slot, lastkb=lastkb):
            pslice = pring[:, slot * QC:(slot + 1) * QC]
            t.matmul(psl[:], onesb[:], pslice, start=kb == 0, stop=lastkb)
            return t.matmul(psoT[qc % 2][:], vt[u % 2][:, kb * P:(kb + 1) * P],
                            pslice, start=kb == 0, stop=lastkb)
        pvd[key] = S.op("te", em, deps=deps)

    def emit_epilogue_evac(u, qc):
        lastkey = (u, qc, kbs(qc) - 1)
        par = qc % 2
        h_lcp = S.op("dve", lambda v, par=par: v.tensor_copy(
            lrow[par][:], psl[:]),
            deps=[pvd[lastkey], last.get(("lrow", par))])
        last[("psl",)] = h_lcp
        h_oev = S.op("dve", lambda v, par=par: v.tensor_copy(
            osb[par][:], psoT[par][:]),
            deps=[pvd[lastkey], last.get(("osb", par))])
        last[("pso", par)] = h_oev
        evac[(u, qc)] = (h_lcp, h_oev)

    def emit_epilogue(u, qc):
        par = qc % 2
        h_lcp, h_oev = evac[(u, qc)]
        tr_l = []
        for j in range(KPQ):
            tr_l.append(S.op("te", lambda t, j=j, par=par: t.transpose(
                pslt[:, j:j + 1], lrow[par][0:1, j * P:(j + 1) * P],
                identf1[0:1, 0:1]),
                deps=[h_lcp, last.get(("pslt",)), last.get(("pstb",))]))
        last[("lrow", par)] = tr_l[-1]
        h_rec = S.op("dve", lambda v, par=par: v.reciprocal(
            recipT[par][:], pslt),
            deps=tr_l + [last.get(("recipT", par))])
        last[("pslt",)] = h_rec
        htr = None
        for j in range(KPQ):
            htr = S.op("te", lambda t, j=j, par=par: t.transpose(
                pstb[:, j * P:(j + 1) * P], osb[par][:, j * P:(j + 1) * P],
                identff[:]),
                deps=[h_oev, h_rec, last.get(("pstb",))])
        nrm = []
        for j in range(KPQ):
            hn = S.op("dve", lambda v, j=j, par=par: v.tensor_scalar_mul(
                osta[par][:, j * P:(j + 1) * P], pstb[:, j * P:(j + 1) * P],
                recipT[par][:, j:j + 1]),
                deps=[htr, h_rec, last.get(("osta", par))])
            nrm.append(hn)
        last[("pstb",)] = nrm[-1]
        last[("osb", par)] = nrm[-1]
        last[("recipT", par)] = nrm[-1]
        sem_name = f"ou{u}_{par}" if unit_sems else f"out{par}"
        if oo.shape and len(oo.shape) == 3:
            dst = oo[u, qc * QC:(qc + 1) * QC, :]
        else:
            dst = oo[u * L + qc * QC:u * L + (qc + 1) * QC, :]
        h_out = S.dma("sync", lambda sy, dst=dst, par=par: sy.dma_start(
            out=dst.rearrange("(t p) d -> p t d", p=P),
            in_=osta[par].ap().rearrange("p (t d) -> p t d", d=P)),
            deps=nrm, sem=sem_name)
        last[("osta", par)] = h_out
        out_handles.append(h_out)
        unit_outs.setdefault(u, []).append(h_out)
        if len(unit_outs[u]) == NQC and unit_end_cb is not None:
            unit_end_cb(u, list(unit_outs[u]))

    out_handles = []
    unit_outs = {}

    LAG = 3
    pending_epi = []

    def after_lpv(pkey):
        pu, pq, pk = pkey
        if pk == kbs(pq) - 1:
            emit_epilogue_evac(pu, pq)
            if bg_hook is not None:
                bg_hook()
            pending_epi.append((pu, pq))
            if pq == NQC - 1:
                unit_done[pu] = pvd[pkey]
                if pu + 2 < U:
                    load_unit(pu + 2)

    for i, key in enumerate(kseq):
        emit_scores(key)
        emit_exp(key)
        if i >= LAG:
            emit_lpv(kseq[i - LAG])
            after_lpv(kseq[i - LAG])
        while pending_epi:
            pu, pq = pending_epi[0]
            if kidx[key] - kidx[(pu, pq, kbs(pq) - 1)] >= LAG + 2:
                emit_epilogue(*pending_epi.pop(0))
            else:
                break
    for i in range(max(0, len(kseq) - LAG), len(kseq)):
        emit_lpv(kseq[i])
        after_lpv(kseq[i])
    for pu, pq in pending_epi:
        emit_epilogue(pu, pq)

    S.op("sync", lambda sy: sy.nop(), deps=out_handles, tick=False)


def build_attn_nc(U=4, L=8192, QC=512, NSLOT=6):
    from contextlib import ExitStack
    nc = bass.Bass("TRN2", target_bir_lowering=False)
    xt = nc.dram_tensor("xt_heads", [U, P, L], F32, kind="ExternalInput").ap()
    xv = nc.dram_tensor("xv_heads", [U, L, P], F32, kind="ExternalInput").ap()
    oo = nc.dram_tensor("o_heads", [U, L, P], F32, kind="ExternalOutput").ap()
    with ExitStack() as ctx:
        S = Sched(nc, ctx)
        block = ctx.enter_context(nc.Block())
        emit_attn(nc, S, ctx, xt, xv, oo, U, L, QC, NSLOT)
        S.flush(block)
    return nc


EPS_TERN = 1e-6
EPS_LN_OUTER = 1e-5
EPS_LN_INNER = 1e-6


def emit_ffn(nc, S, ctx, block, o_rows, x_rows, wfcT, wprojT, y_rows,
             TOK, DM, DF, OB=256, NG=2, EBW=None,
             o_gather=None, gamma_pre=None):
    from contextlib import ExitStack
    e = ctx.enter_context
    ST = TOK // P
    CI = DM // P
    OCH = DF // P
    NOB = DF // OB
    EBW = EBW or min(512, DM)
    NEB = DM // EBW
    STG = ST // NG
    GW = min(DM, DF, 4096)
    NGT = (DM * DF) // (P * GW)
    T2SL = 4
    last = {}

    v_scr = nc.dram_tensor("v_scr", [TOK, DF], BF16).ap()
    t2_scr = nc.dram_tensor("t2_scr", [DF, DM], BF16).ap()
    x1_scr = nc.dram_tensor("x1_scr", [TOK, DM], BF16).ap()

    bar0 = S.barriers()
    identb2 = e(nc.sbuf_tensor("identb2", [P, P], BF16))
    identf2 = e(nc.sbuf_tensor("identf2", [P, P], F32))
    ones_row = e(nc.sbuf_tensor("ones_row", [1, P], F32))
    consts = [_identity(S, identb2[:], deps=bar0), _identity(S, identf2[:], deps=bar0),
              S.op("pool", lambda g: g.memset(ones_row[:], 1.0), deps=bar0)]
    if gamma_pre is None:
        gparam = e(nc.sbuf_tensor("gparam", [P, 6], F32))
    else:
        gparam = gamma_pre[0]
    qparam = e(nc.sbuf_tensor("qparam", [P, 4 * ST], F32))
    t2srow = e(nc.sbuf_tensor("t2srow", [1, DM], BF16))
    junk = e(nc.sbuf_tensor("junk", [P, max(DM, OB)], BF16))
    scol = e(nc.sbuf_tensor("scol", [P, 32], F32))
    from contextlib import ExitStack as _ES
    ctxBC = _ES()
    xhT = ctxBC.enter_context(nc.sbuf_tensor("xhT", [P, CI * TOK], BF16))
    vstats = ctxBC.enter_context(nc.sbuf_tensor("vstats", [P, 2 * ST * NOB], F32))

    def VS(which, st, ob):
        c = which * ST * NOB + st * NOB + ob
        return vstats[:, c:c + 1]

    def QP(which, st):
        c = which * ST + st
        return qparam[:, c:c + 1]

    h_g = [None, None]
    if gamma_pre is not None:
        h_g = list(gamma_pre[1])

    if gamma_pre is None:
      with ExitStack() as ctxA:
        ea = ctxA.enter_context
        gbuf = [ea(nc.sbuf_tensor(f"gbuf{i}", [P, GW], F32)) for i in range(2)]
        gcols = ea(nc.sbuf_tensor("gcols", [P, 2 * NGT], F32))
        gtmp = ea(nc.sbuf_tensor("gtmp", [P, 2], F32))
        grow = ea(nc.sbuf_tensor("grow", [1, P], F32))
        gsc = ea(nc.sbuf_tensor("gsc", [1, 8], F32))
        psga = ea(nc.psum_tensor("psga", [1, P], F32))
        psgb = ea(nc.psum_tensor("psgb", [P, 4], F32))

        hred = {0: [], 1: []}
        for wi, W in ((0, wfcT), (1, wprojT)):
            WT = W.rearrange("(r p) (c w) -> r c p w", p=P, w=GW)
            nr, ncc = W.shape[0] // P, W.shape[1] // GW
            assert nr * ncc == NGT
            for ti in range(nr * ncc):
                slot = ti % 2
                src = WT[ti // ncc, ti % ncc]
                hd = S.dma("sync", lambda sy, s=src, d=gbuf[slot]: sy.dma_start(
                    out=d[:], in_=s), deps=[last.get(("gbuf", slot))],
                    sem=f"g{slot}")
                hr = S.op("dve", lambda v, slot=slot, c=wi * NGT + ti: v.tensor_reduce(
                    gcols[:, c:c + 1], gbuf[slot][:], AX.X, ALU.add,
                    apply_absolute_value=True), deps=[hd])
                last[("gbuf", slot)] = hr
                hred[wi].append(hr)

        for wi in range(2):
            NEL = float(DM * DF)
            h1 = S.op("dve", lambda v, wi=wi: v.tensor_reduce(
                gtmp[:, wi:wi + 1], gcols[:, wi * NGT:(wi + 1) * NGT],
                AX.X, ALU.add), deps=[hred[wi][-1]])
            h2 = S.op("te", lambda t, wi=wi: t.transpose(
                psga[:], gtmp[:, wi:wi + 1], identf2[:]),
                deps=[h1, consts[1], last.get(("psga",))])
            h3 = S.op("dve", lambda v: v.tensor_copy(grow[:], psga[:]),
                      deps=[h2, last.get(("grow",))])
            last[("psga",)] = h3
            h4 = S.op("dve", lambda v, wi=wi: v.tensor_reduce(
                gsc[0:1, wi * 3:wi * 3 + 1], grow[:], AX.X, ALU.add), deps=[h3])
            last[("grow",)] = h4
            h5 = S.op("dve", lambda v, wi=wi, NEL=NEL: v.tensor_scalar_mul(
                gsc[0:1, wi * 3:wi * 3 + 1], gsc[0:1, wi * 3:wi * 3 + 1],
                1.0 / NEL), deps=[h4])
            h6 = S.op("dve", lambda v, wi=wi: v.tensor_scalar(
                gsc[0:1, wi * 3 + 1:wi * 3 + 2], gsc[0:1, wi * 3:wi * 3 + 1],
                EPS_TERN, 0.5, ALU.add, ALU.mult), deps=[h5])
            h7 = S.op("te", lambda t, wi=wi: t.matmul(
                psgb[:, 0:2], ones_row[:], gsc[0:1, wi * 3:wi * 3 + 2],
                start=True, stop=True),
                deps=[h6, consts[2], last.get(("psgb",))])
            h8 = S.op("dve", lambda v, wi=wi: v.tensor_copy(
                gparam[:, wi * 3:wi * 3 + 2], psgb[:, 0:2]), deps=[h7])
            last[("psgb",)] = h8
            h9 = S.op("dve", lambda v, wi=wi: v.tensor_scalar_mul(
                gparam[:, wi * 3 + 2:wi * 3 + 3],
                gparam[:, wi * 3 + 1:wi * 3 + 2], -1.0), deps=[h8])
            h_g[wi] = h9
        S.flush(block)

    def GAM(wi):
        return gparam[:, wi * 3:wi * 3 + 1]

    def TB(wi):
        return gparam[:, wi * 3 + 1:wi * 3 + 2]

    def NTB(wi):
        return gparam[:, wi * 3 + 2:wi * 3 + 3]

    hB_done = []
    h_x1w = {}
    barB = S.barriers()
    with ExitStack() as ctxB:
        eb = ctxB.enter_context
        obuf = ([eb(nc.sbuf_tensor(f"obuf{i}", [P, DM], F32)) for i in range(2)]
                if o_gather is None else [None, None])
        xbuf = [eb(nc.sbuf_tensor(f"xbuf{i}", [P, DM], F32)) for i in range(2)]
        tbuf = [eb(nc.sbuf_tensor(f"tbuf{i}", [P, DM], F32)) for i in range(2)]
        x1b = [eb(nc.sbuf_tensor(f"x1b{i}", [P, DM], BF16)) for i in range(2)]
        ubuf = [eb(nc.sbuf_tensor(f"ubuf{i}", [P, DM], BF16)) for i in range(2)]
        pstr = [eb(nc.psum_tensor(f"pstrB{i}", [P, P], BF16)) for i in range(2)]

        if o_gather is not None:
            obuf_all, h_gath, ctx_ob = o_gather
        for st in range(ST):
            sl = st % 2
            if o_gather is None:
                ho = S.dma("sync", lambda sy, st=st, sl=sl: sy.dma_start(
                    out=obuf[sl][:], in_=o_rows[st * P:(st + 1) * P, :]),
                    deps=[last.get(("obuf", sl)), barB], sem=f"bo{sl}")
            else:
                ho = list(h_gath[st])
            hx = S.dma("sync", lambda sy, st=st, sl=sl: sy.dma_start(
                out=xbuf[sl][:], in_=x_rows[st * P:(st + 1) * P, :]),
                deps=[last.get(("xbuf", sl)), barB], sem=f"bx{sl}")
            osrc = (obuf[sl][:] if o_gather is None
                    else obuf_all[:, st * DM:(st + 1) * DM])
            h_add = S.op("dve", lambda v, sl=sl, osrc=osrc: v.tensor_add(
                tbuf[sl][:], osrc, xbuf[sl][:]),
                deps=[ho, hx, last.get(("tbuf", sl))])
            last[("obuf", sl)] = h_add
            last[("xbuf", sl)] = h_add

            def ln_stats(src, eps, D, c0, dep):
                mcol = scol[:, c0:c0 + 1]
                vcol = scol[:, c0 + 1:c0 + 2]
                rcol = scol[:, c0 + 2:c0 + 3]
                hs1 = S.op("dve", lambda v: v.tensor_reduce(
                    mcol, src, AX.X, ALU.add), deps=[dep, last.get(("scol", c0))])
                hs2 = S.op("act", lambda a: a.activation(
                    out=junk[:, 0:D], in_=src, func=AF.Square, accum_out=vcol),
                    deps=[dep, last.get(("junk",)), last.get(("scol", c0 + 1))])
                last[("junk",)] = hs2
                hm = S.op("dve", lambda v, D=D: v.tensor_scalar_mul(
                    mcol, mcol, 1.0 / D), deps=[hs1])
                hv1 = S.op("dve", lambda v, D=D: v.tensor_scalar_mul(
                    vcol, vcol, 1.0 / D), deps=[hs2])
                hv2 = S.op("dve", lambda v: v.scalar_tensor_tensor(
                    vcol, mcol, mcol, vcol, ALU.mult, ALU.subtract),
                    deps=[hm, hv1])
                hv3 = S.op("dve", lambda v, eps=eps: v.tensor_scalar(
                    vcol, vcol, -1.0, eps, ALU.mult, ALU.add), deps=[hv2])
                hl = S.op("act", lambda s: s.activation(
                    out=rcol, in_=vcol, func=AF.Ln),
                    deps=[hv3, last.get(("scol", c0 + 2))])
                hr = S.op("act", lambda s: s.activation(
                    out=rcol, in_=rcol, func=AF.Exp, scale=-0.5), deps=[hl])
                return hr, mcol, rcol

            hr1, m1, r1 = ln_stats(tbuf[sl][:], EPS_LN_OUTER, DM, 4 * sl, h_add)
            nmr1 = scol[:, 4 * sl + 3:4 * sl + 4]
            h_nmr1 = S.op("dve", lambda v, m1=m1, r1=r1, nmr1=nmr1: v.tensor_scalar(
                nmr1, m1, r1, -1.0, ALU.mult, ALU.mult),
                deps=[hr1, last.get(("scol", 4 * sl + 3))])
            h_x1 = S.op("act", lambda a, sl=sl, r1=r1, nmr1=nmr1: a.activation(
                out=x1b[sl][:], in_=tbuf[sl][:], func=AF.Identity,
                scale=r1, bias=nmr1),
                deps=[h_nmr1, last.get(("x1b", sl))])
            last[("tbuf", sl)] = h_x1
            for c in range(4 * sl, 4 * sl + 4):
                last[("scol", c)] = h_x1
            h_x1w[st] = S.dma("sync", lambda sy, st=st, sl=sl: sy.dma_start(
                out=x1_scr[st * P:(st + 1) * P, :], in_=x1b[sl][:]),
                deps=[h_x1], sem=f"x1w{sl}")

            c2 = 8 + 4 * sl
            qcl = scol[:, c2:c2 + 1]
            rccl = scol[:, c2 + 1:c2 + 2]
            nmrc = scol[:, c2 + 2:c2 + 3]
            vcol1 = scol[:, 4 * sl + 1:4 * sl + 2]
            h_q1 = S.op("dve", lambda v, qcl=qcl, vcol1=vcol1: v.reciprocal(
                qcl, vcol1), deps=[hr1, last.get(("scol", c2))])
            h_q2 = S.op("dve", lambda v, qcl=qcl: v.tensor_scalar(
                qcl, qcl, -EPS_LN_OUTER, 1.0 + EPS_LN_INNER, ALU.mult, ALU.add),
                deps=[h_q1])
            h_q3 = S.op("act", lambda a, qcl=qcl, rccl=rccl: a.activation(
                out=rccl, in_=qcl, func=AF.Ln),
                deps=[h_q2, last.get(("scol", c2 + 1))])
            h_q4 = S.op("act", lambda a, rccl=rccl: a.activation(
                out=rccl, in_=rccl, func=AF.Exp, scale=-0.5), deps=[h_q3])
            h_q5 = S.op("dve", lambda v, rccl=rccl, r1=r1: v.tensor_mul(
                rccl, rccl, r1), deps=[h_q4])
            h_nmrc = S.op("dve", lambda v, m1=m1, rccl=rccl, nmrc=nmrc: v.tensor_scalar(
                nmrc, m1, rccl, -1.0, ALU.mult, ALU.mult),
                deps=[h_q5, last.get(("scol", c2 + 2))])
            h_u = S.op("act", lambda a, sl=sl, rccl=rccl, nmrc=nmrc: a.activation(
                out=ubuf[sl][:], in_=tbuf[sl][:], func=AF.Identity,
                scale=rccl, bias=nmrc),
                deps=[h_nmrc, last.get(("ubuf", sl))])
            for c in (c2, c2 + 1, c2 + 2):
                last[("scol", c)] = h_u
            last[("x1b", sl)] = [h_x1w[st]]
            last[("tbuf", sl)] = [h_x1, h_u]
            h_xh = h_u
            for ci in range(CI):
                tsl = ci % 2
                htr = S.op("te", lambda t, sl=sl, ci=ci, tsl=tsl: t.transpose(
                    pstr[tsl][:], ubuf[sl][:, ci * P:(ci + 1) * P], identb2[:]),
                    deps=[h_xh, consts[0], last.get(("pstrB", tsl))])
                hcp = S.op("act", lambda a, st=st, ci=ci, tsl=tsl: a.copy(
                    xhT[:, ci * TOK + st * P:ci * TOK + (st + 1) * P],
                    pstr[tsl][:]), deps=[htr])
                last[("pstrB", tsl)] = hcp
            last[("ubuf", sl)] = [h_xh, hcp]
            hB_done.append(hcp)
        S.flush(block)
        if o_gather is not None:
            ctx_ob.close()

    h_vw = []
    h_t2w = []
    barC = S.barriers()
    with ExitStack() as ctxC:
        ec = ctxC.enter_context
        NWF = 3
        wf = [ec(nc.sbuf_tensor(f"wf{i}", [P, CI * OB], F32)) for i in range(NWF)]
        t1b = [ec(nc.sbuf_tensor(f"t1b{i}", [P, CI * OB], BF16)) for i in range(3)]
        bbuf = ec(nc.sbuf_tensor("bbuf", [P, max(CI * OB, DM)], BF16))
        bbuf3 = ec(nc.sbuf_tensor("bbuf3", [P, DM], BF16))
        vstage = [ec(nc.sbuf_tensor(f"vstage{i}", [P, ST * OB], BF16)) for i in range(2)]
        wp = [ec(nc.sbuf_tensor(f"wp{i}", [P, DM], F32)) for i in range(2)]
        ones_c = ec(nc.sbuf_tensor("ones_c", [P, 1], BF16))
        psT2 = [ec(nc.psum_tensor(f"psT2_{i}", [1, min(512, DM)], F32))
                for i in range(max(1, DM // 512))]
        h_onesc = S.op("pool", lambda g: g.memset(ones_c[:], 1.0), deps=barC)
        t2b = [ec(nc.sbuf_tensor(f"t2b{i}", [P, DM], BF16)) for i in range(3)]
        psh = [ec(nc.psum_tensor(f"psh{i}", [P, OB], F32))[:] for i in range(4)]

        wfT = wfcT.rearrange("(ci p) o -> p ci o", p=P)
        wpT = wprojT.rearrange("(och p) e -> och p e", p=P)
        v_scrT = v_scr.rearrange("(st p) o -> p st o", p=P)

        t2_state = {"tern": 0, "sum": 0, "h": {}}

        def emit_t2_tern(och):
            if och >= OCH:
                return
            t2_state["tern"] = och + 1
            sl = och % 3
            wsl = och % 2
            hd = S.dma("pool", lambda sy, och=och, wsl=wsl: sy.dma_start(
                out=wp[wsl][:], in_=wpT[och]),
                deps=[last.get(("wp", wsl)), barC], sem=f"wp{wsl}")
            hb = S.op("dve", lambda v, wsl=wsl: v.tensor_scalar(
                bbuf3[:, 0:DM], wp[wsl][:], NTB(1), None, ALU.is_lt),
                deps=[hd, h_g[1], last.get(("bbuf3",))])
            ht = S.op("dve", lambda v, sl=sl, wsl=wsl: v.scalar_tensor_tensor(
                t2b[sl][:], wp[wsl][:], TB(1), bbuf3[:, 0:DM],
                ALU.is_gt, ALU.subtract),
                deps=[hb, last.get(("t2b", sl))])
            last[("bbuf3",)] = ht
            last[("wp", wsl)] = ht
            t2_state["h"][och] = ht

        def emit_t2_sum(och):
            if och >= OCH:
                return
            t2_state["sum"] = och + 1
            sl = och % 3
            ht = t2_state["h"][och]
            EW = min(512, DM)

            def emsum(t, och=och, sl=sl, EW=EW):
                for j in range(DM // EW):
                    mm = t.matmul(psT2[j][0:1, :], ones_c[:],
                                  t2b[sl][:, j * EW:(j + 1) * EW],
                                  start=och == 0, stop=och == OCH - 1)
                return mm
            hts = S.op("te", emsum, deps=[ht, h_onesc])
            t2_state["hsum"] = hts
            hw = S.dma("pool", lambda sy, och=och, sl=sl: sy.dma_start(
                out=t2_scr[och * P:(och + 1) * P, :], in_=t2b[sl][:]),
                deps=[ht, hts], sem=f"t2w{sl}")
            last[("t2b", sl)] = hw
            h_t2w.append(hw)

        def emit_t2(n):
            for _ in range(n):
                while t2_state["tern"] < min(OCH, t2_state["sum"] + 3):
                    emit_t2_tern(t2_state["tern"])
                if t2_state["sum"] < OCH:
                    emit_t2_sum(t2_state["sum"])

        tern_h = {}

        def emit_tern(ob):
            if ob >= NOB or ob in tern_h:
                return
            sl = ob % NWF
            slw = ob % 3
            hwf = S.dma("sync", lambda sy, ob=ob, sl=sl: sy.dma_start(
                out=wf[sl].ap().rearrange("p (ci o) -> p ci o", o=OB),
                in_=wfT[:, :, ob * OB:(ob + 1) * OB]),
                deps=[last.get(("wf", sl)), barC], sem=f"wf{sl}")
            hb1 = S.op("dve", lambda v, sl=sl: v.tensor_scalar(
                bbuf[:, 0:CI * OB], wf[sl][:], NTB(0), None, ALU.is_lt),
                deps=[hwf, h_g[0], last.get(("bbuf",))])
            ht1 = S.op("dve", lambda v, sl=sl, slw=slw: v.scalar_tensor_tensor(
                t1b[slw][:], wf[sl][:], TB(0), bbuf[:, 0:CI * OB],
                ALU.is_gt, ALU.subtract),
                deps=[hb1, last.get(("t1b", slw))])
            last[("wf", sl)] = ht1
            last[("bbuf",)] = ht1
            tern_h[ob] = ht1

        emit_tern(0)
        emit_tern(1)
        emit_tern(2)
        for ob in range(NOB):
            sl2 = ob % 2
            slw = ob % 3
            ht1 = tern_h[ob]
            lastmm_ob = None

            for st in range(ST):
                hsl = st % 4
                deps = [ht1, hB_done[st], last.get(("psh", hsl))]

                def em(t, st=st, slw=slw, hsl=hsl):
                    for ci in range(CI):
                        mm = t.matmul(psh[hsl],
                                      xhT[:, ci * TOK + st * P:ci * TOK + (st + 1) * P],
                                      t1b[slw][:, ci * OB:(ci + 1) * OB],
                                      start=ci == 0, stop=ci == CI - 1)
                    return mm
                hmm = S.op("te", em, deps=deps)
                lastmm_ob = hmm
                hge = S.op("act", lambda s, st=st, ob=ob, sl2=sl2, hsl=hsl: s.activation(
                    out=vstage[sl2][:, st * OB:(st + 1) * OB], in_=psh[hsl],
                    func=AF.Gelu, scale=GAM(0), accum_out=VS(0, st, ob)),
                    deps=[hmm, last.get(("vstage", sl2)), h_g[0]])
                last[("psh", hsl)] = hge
                hsq = S.op("act", lambda a, st=st, ob=ob, sl2=sl2: a.activation(
                    out=junk[:, 0:OB], in_=vstage[sl2][:, st * OB:(st + 1) * OB],
                    func=AF.Square, accum_out=VS(1, st, ob)),
                    deps=[hge, last.get(("junk",))])
                last[("junk",)] = hsq
                last.setdefault(("vst_done", sl2), [])
                if st == 0:
                    last[("vst_done", sl2)] = []
                last[("vst_done", sl2)] += [hge, hsq]
            last[("t1b", slw)] = lastmm_ob
            emit_tern(ob + 3)
            hvw = S.dma("pool", lambda sy, ob=ob, sl2=sl2: sy.dma_start(
                out=v_scrT[:, :, ob * OB:(ob + 1) * OB],
                in_=vstage[sl2].ap().rearrange("p (st o) -> p st o", o=OB)),
                deps=list(last[("vst_done", sl2)]), sem=f"vw{sl2}")
            last[("vstage", sl2)] = hvw
            h_vw.append(hvw)
            emit_t2(max(1, OCH // NOB + 1))
        emit_t2(OCH)
        EW = min(512, DM)
        h_t2row = []
        for j in range(DM // EW):
            h_t2row.append(S.op("dve", lambda v, j=j, EW=EW: v.tensor_copy(
                t2srow[0:1, j * EW:(j + 1) * EW], psT2[j][0:1, :]),
                deps=[t2_state["hsum"]]))
        S.flush(block)

    for st in range(ST):
        hs1 = S.op("dve", lambda v, st=st: v.tensor_reduce(
            QP(0, st), vstats[:, 0 * ST * NOB + st * NOB:0 * ST * NOB + (st + 1) * NOB],
            AX.X, ALU.add), deps=h_vw[-2:])
        hs2 = S.op("dve", lambda v, st=st: v.tensor_reduce(
            QP(1, st), vstats[:, 1 * ST * NOB + st * NOB:1 * ST * NOB + (st + 1) * NOB],
            AX.X, ALU.add), deps=h_vw[-2:])
        hm = S.op("dve", lambda v, st=st: v.tensor_scalar_mul(
            QP(0, st), QP(0, st), 1.0 / DF), deps=[hs1])
        hv1 = S.op("dve", lambda v, st=st: v.tensor_scalar_mul(
            QP(1, st), QP(1, st), 1.0 / DF), deps=[hs2])
        hv2 = S.op("dve", lambda v, st=st: v.scalar_tensor_tensor(
            QP(1, st), QP(0, st), QP(0, st), QP(1, st), ALU.mult, ALU.subtract),
            deps=[hm, hv1])
        hv3 = S.op("dve", lambda v, st=st: v.tensor_scalar(
            QP(1, st), QP(1, st), -1.0, EPS_LN_INNER, ALU.mult, ALU.add),
            deps=[hv2])
        hl = S.op("act", lambda s, st=st: s.activation(
            out=QP(1, st), in_=QP(1, st), func=AF.Ln), deps=[hv3])
        hr = S.op("act", lambda s, st=st: s.activation(
            out=QP(1, st), in_=QP(1, st), func=AF.Exp, scale=-0.5), deps=[hl])
        hg2 = S.op("dve", lambda v, st=st: v.tensor_mul(
            QP(2, st), QP(1, st), GAM(1)), deps=[hr, h_g[1]])
        hg3 = S.op("dve", lambda v, st=st: v.tensor_scalar(
            QP(3, st), QP(0, st), QP(2, st), -1.0, ALU.mult, ALU.mult),
            deps=[hg2, hm])
        last[("qp", st)] = hg3

    ctxBC.close()

    h_out = []
    vhT = e(nc.sbuf_tensor("vhT", [P, OCH * STG * P], BF16))
    t2p = [e(nc.sbuf_tensor(f"t2p{i}", [P, EBW], BF16)) for i in range(T2SL)]
    x2pre = e(nc.sbuf_tensor("x2pre", [P, STG * DM], BF16))
    x1t = [e(nc.sbuf_tensor(f"x1t{i}", [P, DM], BF16)) for i in range(2)]
    tt2 = e(nc.sbuf_tensor("tt2", [P, DM], F32))
    outst = [e(nc.sbuf_tensor("outst0", [P, DM], F32))]

    h_vw_all = (S.dma_sems.get("vw0") and None) or None
    def grand(sem_prefix, nslots=2):
        hs = []
        for i in range(nslots):
            nm = f"{sem_prefix}{i}"
            if nm in S.dma_sems:
                hs.append((S.dma_sems[nm], S.dma_cnt[nm]))
        return hs

    h_v_all = grand("vw")
    h_t2_all = grand("t2w")
    h_x1_all = grand("x1w")
    barD = S.barriers()
    h_v_all += barD
    h_t2_all += barD
    h_x1_all += barD

    T2B = e(nc.sbuf_tensor("T2B", [P, DM], F32))
    with _ES() as ctxTB:
        pbc = ctxTB.enter_context(nc.psum_tensor("pbcD", [P, min(512, DM)], F32))
        onesrD = ctxTB.enter_context(nc.sbuf_tensor("onesrD", [1, P], BF16))
        h_or = S.op("pool", lambda g: g.memset(onesrD[:], 1.0), deps=barD)
        EW = min(512, DM)
        for j in range(DM // EW):
            hbm = S.op("te", lambda t, j=j, EW=EW: t.matmul(
                pbc[:], onesrD[:], t2srow[0:1, j * EW:(j + 1) * EW],
                start=True, stop=True),
                deps=[h_or, last.get(("pbcD",))] + h_t2row + barD)
            hbc = S.op("dve", lambda v, j=j, EW=EW: v.tensor_copy(
                T2B[:, j * EW:(j + 1) * EW], pbc[:]), deps=[hbm])
            last[("pbcD",)] = hbc
        last[("T2B",)] = hbc
        S.flush(block)

    psy = [e(nc.psum_tensor(f"psy{i}", [P, EBW], F32)) for i in range(STG)]
    for g in range(NG):
        trs = []
        for och in range(OCH):
            trs.append(S.dma("sync", lambda sy, g=g, och=och: sy.dma_start_transpose(
                out=vhT[:, och * STG * P:(och + 1) * STG * P],
                in_=v_scr[g * STG * P:(g + 1) * STG * P,
                          och * P:(och + 1) * P]),
                deps=h_v_all + [last.get(("vhT", och))], sem=f"vt{och % 4}"))
        bar_build = trs

        if True:
            for eb in range(NEB):
                for och in range(OCH):
                    tsl = (eb * OCH + och) % T2SL
                    hd = S.dma("sync", lambda sy, eb=eb, och=och, tsl=tsl: sy.dma_start(
                        out=t2p[tsl][:],
                        in_=t2_scr[och * P:(och + 1) * P, eb * EBW:(eb + 1) * EBW]),
                        deps=h_t2_all + [last.get(("t2p", tsl))], sem=f"t2r{tsl}")

                    def em(t, eb=eb, och=och, tsl=tsl, g=g):
                        for st8 in range(STG):
                            mm = t.matmul(
                                psy[st8][:],
                                vhT[:, och * STG * P + st8 * P:och * STG * P + (st8 + 1) * P],
                                t2p[tsl][:], start=och == 0, stop=och == OCH - 1)
                        return mm
                    deps = [hd, trs[och]]
                    if och == 0:
                        deps += [last.get(("psy", st8)) for st8 in range(STG)]
                        deps.append(last.get(("T2B",)))
                    hmm = S.op("te", em, deps=deps)
                    last[("t2p", tsl)] = hmm
                    if eb == NEB - 1:
                        last[("vhT", och)] = hmm
                for st8 in range(STG):
                    st = g * STG + st8
                    hev = S.op("dve", lambda v, st8=st8, eb=eb, st=st: v.tensor_scalar_mul(
                        x2pre[:, st8 * DM + eb * EBW:st8 * DM + (eb + 1) * EBW],
                        psy[st8][:], QP(2, st)),
                        deps=[hmm, last.get(("qp", st)), last.get(("x2pre", st8, eb))])
                    last[("psy", st8)] = hev
                    last[("evac", g, eb, st8)] = hev
            for st8 in range(STG):
                st = g * STG + st8
                sl = st % 2
                hx1 = S.dma("sync", lambda sy, st=st, sl=sl: sy.dma_start(
                    out=x1t[sl][:], in_=x1_scr[st * P:(st + 1) * P, :]),
                    deps=h_x1_all + [last.get(("x1t", sl))], sem=f"x1r{sl}")
                evdeps = [last[("evac", g, eb, st8)] for eb in range(NEB)]
                hadd0 = S.op("dve", lambda v, st=st, sl=sl: v.scalar_tensor_tensor(
                    tt2[:], T2B[:], QP(3, st), x1t[sl][:], ALU.mult, ALU.add),
                    deps=[hx1, last.get(("qp", st)), last.get(("tt2",))] + (
                        [last.get(("T2B",))] if last.get(("T2B",)) else []))
                hadd = S.op("dve", lambda v, st8=st8: v.tensor_add(
                    tt2[:], tt2[:], x2pre[:, st8 * DM:(st8 + 1) * DM]),
                    deps=[hadd0] + evdeps)
                last[("x1t", sl)] = hadd
                mcol = scol[:, 20:21]
                vcol = scol[:, 21:22]
                rcol = scol[:, 22:23]
                hs1 = S.op("dve", lambda v: v.tensor_reduce(
                    tt2[:, 0:1] if False else mcol, tt2[:], AX.X, ALU.add),
                    deps=[hadd, last.get(("scolD",))])
                hs2 = S.op("dve", lambda v: v.scalar_tensor_tensor(
                    junk[:, 0:DM], tt2[:], 1.0, tt2[:], ALU.mult, ALU.mult,
                    accum_out=vcol), deps=[hadd, last.get(("junk",))])
                last[("junk",)] = hs2
                hm = S.op("dve", lambda v: v.tensor_scalar_mul(
                    mcol, mcol, 1.0 / DM), deps=[hs1])
                hv1 = S.op("dve", lambda v: v.tensor_scalar_mul(
                    vcol, vcol, 1.0 / DM), deps=[hs2])
                hv2 = S.op("dve", lambda v: v.scalar_tensor_tensor(
                    vcol, mcol, mcol, vcol, ALU.mult, ALU.subtract),
                    deps=[hm, hv1])
                hv3 = S.op("dve", lambda v: v.tensor_scalar(
                    vcol, vcol, -1.0, EPS_LN_OUTER, ALU.mult, ALU.add),
                    deps=[hv2])
                hl = S.op("act", lambda s: s.activation(
                    out=rcol, in_=vcol, func=AF.Ln), deps=[hv3])
                hr = S.op("act", lambda s: s.activation(
                    out=rcol, in_=rcol, func=AF.Exp, scale=-0.5), deps=[hl])
                osl = 0
                happ = S.op("dve", lambda v, osl=osl: v.tensor_scalar(
                    outst[osl][:], tt2[:], mcol, rcol, ALU.subtract, ALU.mult),
                    deps=[hr, last.get(("outst", osl))])
                last[("tt2",)] = happ
                last[("scolD",)] = happ
                ho = S.dma("sync", lambda sy, st=st, osl=osl: sy.dma_start(
                    out=y_rows[st * P:(st + 1) * P, :], in_=outst[osl][:]),
                    deps=[happ], sem=f"yw{osl}")
                last[("outst", osl)] = ho
                last[("x2pre_free", st8)] = happ
                for eb in range(NEB):
                    last[("x2pre", st8, eb)] = happ
                h_out.append(ho)
            S.flush(block)

    S.op("sync", lambda sy: sy.nop(), deps=h_out, tick=False)
    S.flush(block)


def build_ffn_nc(TOK=2048, DM=2048, DF=8192, OB=256, NG=2, EBW=None):
    from contextlib import ExitStack
    nc = bass.Bass("TRN2", target_bir_lowering=False)
    o_rows = nc.dram_tensor("o_rows", [TOK, DM], F32, kind="ExternalInput").ap()
    x_rows = nc.dram_tensor("x_rows", [TOK, DM], F32, kind="ExternalInput").ap()
    wfcT = nc.dram_tensor("wfcT", [DM, DF], F32, kind="ExternalInput").ap()
    wprojT = nc.dram_tensor("wprojT", [DF, DM], F32, kind="ExternalInput").ap()
    y_rows = nc.dram_tensor("y_rows", [TOK, DM], F32, kind="ExternalOutput").ap()
    with ExitStack() as ctx:
        S = Sched(nc, ctx)
        block = ctx.enter_context(nc.Block())
        emit_ffn(nc, S, ctx, block, o_rows, x_rows, wfcT, wprojT, y_rows,
                 TOK, DM, DF, OB, NG, EBW)
    return nc


NCORE = 8
B_, L_, DM_, DF_, H_ = 2, 8192, 2048, 8192, 16
UPC = (B_ * H_) // NCORE
TPC = (B_ * L_) // NCORE


def kernel(L0_attn_x, L0_ffn_w_fc, L0_ffn_w_proj):
    return kernel_fused(L0_attn_x, L0_ffn_w_fc, L0_ffn_w_proj)


def kernel_v0(L0_attn_x, L0_ffn_w_fc, L0_ffn_w_proj):
    x = np.ascontiguousarray(np.asarray(L0_attn_x, dtype=np.float32))
    wfcT = np.ascontiguousarray(np.asarray(L0_ffn_w_fc, np.float32).T)
    wprojT = np.ascontiguousarray(np.asarray(L0_ffn_w_proj, np.float32).T)

    nc1 = build_attn_nc(U=UPC, L=L_, QC=512, NSLOT=8)
    in1 = []
    for c in range(NCORE):
        xts, xvs = [], []
        for j in range(UPC):
            u = UPC * c + j
            b, h = divmod(u, H_)
            sl = x[b, :, h * P:(h + 1) * P]
            xvs.append(sl)
            xts.append(sl.T)
        in1.append({"xt_heads": np.ascontiguousarray(np.stack(xts)),
                    "xv_heads": np.ascontiguousarray(np.stack(xvs))})
    r1 = run_bass_kernel_spmd(nc1, in1, core_ids=list(range(NCORE)))

    o_flat = np.empty((B_ * L_, DM_), np.float32)
    for c in range(NCORE):
        oh = r1.results[c]["o_heads"]
        for j in range(UPC):
            u = UPC * c + j
            b, h = divmod(u, H_)
            o_flat[b * L_:(b + 1) * L_, h * P:(h + 1) * P] = oh[j]

    x_flat = x.reshape(B_ * L_, DM_)
    nc2 = build_ffn_nc(TOK=TPC, DM=DM_, DF=DF_, OB=256, NG=2, EBW=512)
    in2 = [{"o_rows": np.ascontiguousarray(o_flat[c * TPC:(c + 1) * TPC]),
            "x_rows": np.ascontiguousarray(x_flat[c * TPC:(c + 1) * TPC]),
            "wfcT": wfcT, "wprojT": wprojT} for c in range(NCORE)]
    r2 = run_bass_kernel_spmd(nc2, in2, core_ids=list(range(NCORE)))

    y = np.concatenate([r2.results[c]["y_rows"] for c in range(NCORE)], axis=0)
    return y.reshape(B_, L_, DM_)


def emit_gamma(nc, S, ctx, wfcT, wprojT, gparam, ones_row, identf_ap,
               DM, DF, id_dep):
    e = ctx.enter_context
    GW = min(DM, DF, 4096)
    NGT = (DM * DF) // (P * GW)
    gbuf = [e(nc.sbuf_tensor(f"gbufF{i}", [P, GW], F32)) for i in range(2)]
    gcols = e(nc.sbuf_tensor("gcolsF", [P, 2 * NGT], F32))
    gtmp = e(nc.sbuf_tensor("gtmpF", [P, 2], F32))
    grow = e(nc.sbuf_tensor("growF", [1, P], F32))
    gsc = e(nc.sbuf_tensor("gscF", [1, 8], F32))
    last = {}
    hred = {0: [], 1: []}
    tile_fns = []

    for wi, W in ((0, wfcT), (1, wprojT)):
        WT = W.rearrange("(r p) (c w) -> r c p w", p=P, w=GW)
        nr, ncc = W.shape[0] // P, W.shape[1] // GW
        assert nr * ncc == NGT
        for ti in range(nr * ncc):
            def tile(wi=wi, ti=ti, WT=WT, ncc=ncc):
                slot = (len(hred[0]) + len(hred[1])) % 2
                src = WT[ti // ncc, ti % ncc]
                hd = S.dma("sync", lambda sy, src=src, slot=slot: sy.dma_start(
                    out=gbuf[slot][:], in_=src),
                    deps=[last.get(("gbuf", slot))], sem=f"gF{slot}")
                hr = S.op("dve", lambda v, slot=slot, c=wi * NGT + ti: v.tensor_reduce(
                    gcols[:, c:c + 1], gbuf[slot][:], AX.X, ALU.add,
                    apply_absolute_value=True), deps=[hd])
                last[("gbuf", slot)] = hr
                hred[wi].append(hr)
            tile_fns.append(tile)

    h_g = [None, None]

    def finalize(extra_deps=()):
        psga = e(nc.psum_tensor("psgaF", [1, P], F32))
        psgb = e(nc.psum_tensor("psgbF", [P, 4], F32))
        for wi in range(2):
            NEL = float(DM * DF)
            h1 = S.op("dve", lambda v, wi=wi: v.tensor_reduce(
                gtmp[:, wi:wi + 1], gcols[:, wi * NGT:(wi + 1) * NGT],
                AX.X, ALU.add), deps=[hred[wi][-1]] + list(extra_deps))
            h2 = S.op("te", lambda t, wi=wi: t.transpose(
                psga[:], gtmp[:, wi:wi + 1], identf_ap),
                deps=[h1, id_dep, last.get(("psga",))] + list(extra_deps))
            h3 = S.op("dve", lambda v: v.tensor_copy(grow[:], psga[:]),
                      deps=[h2, last.get(("grow",))])
            last[("psga",)] = h3
            h4 = S.op("dve", lambda v, wi=wi: v.tensor_reduce(
                gsc[0:1, wi * 3:wi * 3 + 1], grow[:], AX.X, ALU.add), deps=[h3])
            last[("grow",)] = h4
            h5 = S.op("dve", lambda v, wi=wi, NEL=NEL: v.tensor_scalar_mul(
                gsc[0:1, wi * 3:wi * 3 + 1], gsc[0:1, wi * 3:wi * 3 + 1],
                1.0 / NEL), deps=[h4])
            h6 = S.op("dve", lambda v, wi=wi: v.tensor_scalar(
                gsc[0:1, wi * 3 + 1:wi * 3 + 2], gsc[0:1, wi * 3:wi * 3 + 1],
                EPS_TERN, 0.5, ALU.add, ALU.mult), deps=[h5])
            h7 = S.op("te", lambda t, wi=wi: t.matmul(
                psgb[:, 0:2], ones_row[:], gsc[0:1, wi * 3:wi * 3 + 2],
                start=True, stop=True),
                deps=[h6, last.get(("psgb",))])
            h8 = S.op("dve", lambda v, wi=wi: v.tensor_copy(
                gparam[:, wi * 3:wi * 3 + 2], psgb[:, 0:2]), deps=[h7])
            last[("psgb",)] = h8
            h9 = S.op("dve", lambda v, wi=wi: v.tensor_scalar_mul(
                gparam[:, wi * 3 + 2:wi * 3 + 3],
                gparam[:, wi * 3 + 1:wi * 3 + 2], -1.0), deps=[h8])
            h_g[wi] = h9

    return tile_fns, finalize, h_g


def build_fused_nc(U=4, L=8192, QC=512, NSLOT=8, TOK=2048, DM=2048, DF=8192,
                   OB=256, NG=2, EBW=512):
    from contextlib import ExitStack
    nc = bass.Bass("TRN2", target_bir_lowering=False)
    xt = nc.dram_tensor("xt_heads", [U, P, L], F32, kind="ExternalInput").ap()
    xv = nc.dram_tensor("xv_heads", [U, L, P], F32, kind="ExternalInput").ap()
    x_rows = nc.dram_tensor("x_rows", [TOK, DM], F32, kind="ExternalInput").ap()
    wfcT = nc.dram_tensor("wfcT", [DM, DF], F32, kind="ExternalInput").ap()
    wprojT = nc.dram_tensor("wprojT", [DF, DM], F32, kind="ExternalInput").ap()
    oidx = nc.dram_tensor("oidx", [P, 64], I32, kind="ExternalInput").ap()
    y_rows = nc.dram_tensor("y_rows", [TOK, DM], F32, kind="ExternalOutput").ap()
    o_cc = nc.dram_tensor("o_cc", [U * L, P], BF16).ap()
    o_g = [nc.dram_tensor(f"o_g{u}", [8 * L, P], BF16, addr_space="Shared").ap()
           for u in range(U)]

    ST = TOK // P
    TPB = ST // 2
    NIC = 8 * TPB
    NH = DM // P
    with ExitStack() as ctx:
        S = Sched(nc, ctx)
        block = ctx.enter_context(nc.Block())
        e = ctx.enter_context

        identfF = e(nc.sbuf_tensor("identfF", [P, P], F32))
        ones_rowF = e(nc.sbuf_tensor("ones_rowF", [1, P], F32))
        gparamF = e(nc.sbuf_tensor("gparamF", [P, 6], F32))
        ctx_ob = ExitStack()
        obuf_all = ctx_ob.enter_context(
            nc.sbuf_tensor("obuf_all", [P, ST * DM], BF16, side="right"))
        oidx_sb = e(nc.sbuf_tensor("oidx_sb", [P, NIC], I32))
        h_idf = _identity(S, identfF[:])
        h_or = S.op("pool", lambda g: g.memset(ones_rowF[:], 1.0))
        h_oidx = S.dma("sync", lambda sy: sy.dma_start(
            out=oidx_sb[:], in_=oidx[:, 0:NIC]), sem="oidx")

        h_cc = []
        h_gath = []
        h_gath_st = {}

        with ExitStack() as ctx_gamma:
            tile_fns, gamma_finalize, h_g = emit_gamma(
                nc, S, ctx_gamma, wfcT, wprojT, gparamF, ones_rowF,
                identfF[:], DM, DF, h_idf)
            tile_iter = iter(tile_fns)

            def bg_hook():
                fn = next(tile_iter, None)
                if fn is not None:
                    fn()

            def unit_end_cb(u, outs):
                h = S.op("pool", lambda g, u=u: g.collective_compute(
                    "AllGather", ALU.bypass,
                    replica_groups=[list(range(8))],
                    ins=[o_cc[u * L:(u + 1) * L, :].opt()],
                    outs=[o_g[u].opt()]), deps=outs)
                h_cc.append(h)
                prevg = None
                for st in range(ST):
                    b = st // TPB
                    for hh in range(NH):
                        g_u = b * NH + hh
                        if g_u % U != u:
                            continue
                        r = g_u // U
                        icol = r * TPB + (st % TPB)
                        prevg = S.dma("pool", lambda g, u=u, st=st, hh=hh,
                                      icol=icol: g.indirect_dma_start(
                            out=obuf_all[:, st * DM + hh * P:st * DM + (hh + 1) * P],
                            out_offset=None,
                            in_=o_g[u][:, :],
                            in_offset=IndirectOffsetOnAxis(
                                ap=oidx_sb[:, icol:icol + 1], axis=0)),
                            deps=[h, h_oidx, prevg], sem=f"og{u}")
                        h_gath.append(prevg)
                    h_gath_st.setdefault(st, []).append(prevg)

            with ExitStack() as ctx_attn:
                emit_attn(nc, S, ctx_attn, xt, xv, o_cc, U, L, QC, NSLOT,
                          o_bf16=True, unit_sems=True, bg_hook=bg_hook,
                          unit_end_cb=unit_end_cb)
                for fn in tile_iter:
                    fn()
                S.flush(block)
            gamma_finalize(extra_deps=S.barriers())
            S.flush(block)

        emit_ffn(nc, S, ctx, block, None, x_rows, wfcT, wprojT, y_rows,
                 TOK, DM, DF, OB, NG, EBW,
                 o_gather=(obuf_all, h_gath_st, ctx_ob),
                 gamma_pre=(gparamF, h_g))
    return nc


def kernel_fused(L0_attn_x, L0_ffn_w_fc, L0_ffn_w_proj):
    x = np.ascontiguousarray(np.asarray(L0_attn_x, dtype=np.float32))
    wfcT = np.ascontiguousarray(np.asarray(L0_ffn_w_fc, np.float32).T)
    wprojT = np.ascontiguousarray(np.asarray(L0_ffn_w_proj, np.float32).T)

    nc = build_fused_nc(U=UPC, L=L_, QC=512, NSLOT=8, TOK=TPC, DM=DM_, DF=DF_,
                        OB=256, NG=2, EBW=512)
    SPC = L_ // NCORE
    in_maps = []
    for c in range(NCORE):
        xts, xvs = [], []
        for j in range(UPC):
            u = UPC * c + j
            b, h = divmod(u, H_)
            sl = x[b, :, h * P:(h + 1) * P]
            xvs.append(sl)
            xts.append(sl.T)
        xr = np.concatenate([x[0, c * SPC:(c + 1) * SPC, :],
                             x[1, c * SPC:(c + 1) * SPC, :]], axis=0)
        oidx = np.empty((P, 64), np.int32)
        pp = np.arange(P, dtype=np.int32)
        for r in range(8):
            for srow in range(8):
                oidx[:, r * 8 + srow] = r * L_ + c * SPC + srow * P + pp
        in_maps.append({"xt_heads": np.ascontiguousarray(np.stack(xts)),
                        "xv_heads": np.ascontiguousarray(np.stack(xvs)),
                        "x_rows": np.ascontiguousarray(xr),
                        "wfcT": wfcT, "wprojT": wprojT, "oidx": oidx})
    res = run_bass_kernel_spmd(nc, in_maps, core_ids=list(range(NCORE)))
    out = np.empty((B_, L_, DM_), np.float32)
    for c in range(NCORE):
        y = res.results[c]["y_rows"]
        out[0, c * SPC:(c + 1) * SPC, :] = y[:SPC]
        out[1, c * SPC:(c + 1) * SPC, :] = y[SPC:]
    return out
